# version 10
# baseline (speedup 1.0000x reference)
import numpy as np
from contextlib import ExitStack

import concourse.bass as bass
import concourse.mybir as mybir
from concourse.bass_utils import run_bass_kernel_spmd

F32 = mybir.dt.float32
BF16 = mybir.dt.bfloat16
AF = mybir.ActivationFunctionType
ALU = mybir.AluOpType
AX = mybir.AxisListType

D = 4096
T = 1024
NI = D // 128
NH = 8
NCORES = 8
EPS_LN = 1e-5
EPS_DEN = 1e-6


class Step:
    __slots__ = ("eng", "sem", "val")

    def __init__(self, eng, sem, val):
        self.eng, self.sem, self.val = eng, sem, val


class KB:

    ENGS = ("sync", "tensor", "vector", "scalar", "gpsimd")

    def __init__(self):
        self.sched = []
        self.count = {}
        self.bufs = {}

    def _b(self, name):
        if name not in self.bufs:
            self.bufs[name] = {"w": None, "pend": []}
        return self.bufs[name]

    def rdep(self, *names):
        return [self._b(n)["w"] for n in names]

    def wdep(self, *names):
        out = []
        for n in names:
            b = self._b(n)
            out += b["pend"]
            out.append(b["w"])
        return out

    def did_read(self, step, *names):
        for n in names:
            self._b(n)["pend"].append(step)

    def did_write(self, step, *names):
        for n in names:
            b = self._b(n)
            b["w"] = step
            b["pend"] = []

    def op(self, eng, fn, deps=(), reads=(), writes=(), sem=None, amt=1):
        if sem is None:
            sem = {"tensor": "et", "vector": "ev", "scalar": "es",
                   "gpsimd": "ep", "sync": "ey"}[eng]
        alldeps = list(deps) + self.rdep(*reads) + self.wdep(*writes)
        waits = []
        for d in alldeps:
            if d is None:
                continue
            if isinstance(d, (list, tuple)):
                waits += [(x.sem, x.val) for x in d if x is not None]
            else:
                waits.append((d.sem, d.val))
        self.count[sem] = self.count.get(sem, 0) + amt
        st = Step(eng, sem, self.count[sem])
        self.sched.append((eng, waits, fn, sem, amt))
        self.did_read(st, *reads)
        self.did_write(st, *writes)
        return st

    def dma(self, eng, fn, deps=(), reads=(), writes=(), sem="dy"):
        return self.op(eng, fn, deps, reads, writes, sem=sem, amt=16)

    def emit(self, nc, sems):
        with nc.Block() as block:
            for eng_name in self.ENGS:
                entries = [e for e in self.sched if e[0] == eng_name]
                if not entries:
                    continue

                def section(eng, entries=entries):
                    waited = {}
                    for (_, waits, fn, sem, amt) in entries:
                        best = {}
                        for (s, v) in waits:
                            if v > best.get(s, -1):
                                best[s] = v
                        for s, v in sorted(best.items()):
                            if waited.get(s, 0) < v:
                                eng.wait_ge(sems[s], v)
                                waited[s] = v
                        inst = fn(eng)
                        if inst is not None:
                            inst.then_inc(sems[sem], amt)

                getattr(block, eng_name)(section)


def build(debug=False):
    nc = bass.Bass("TRN2", target_bir_lowering=False, debug=False, num_devices=8)
    kb = KB()

    xT_d = nc.dram_tensor("xT", [D, T], F32, kind="ExternalInput")
    wq_d = nc.dram_tensor("wq_st", [4, NI, 128, 256], F32, kind="ExternalInput")
    wo_d = nc.dram_tensor("wo_st", [16, 8, 4, 128, 256], F32, kind="ExternalInput")
    wt_d = nc.dram_tensor("wt_st", [16, NI, 128, 256], F32, kind="ExternalInput")
    bq_d = nc.dram_tensor("bq_pm", [128, 8], F32, kind="ExternalInput")
    b_d = nc.dram_tensor("b_pm", [128, 32], F32, kind="ExternalInput")
    zT_d = nc.dram_tensor("zT", [128, 8], F32, kind="ExternalInput")
    M_d = nc.dram_tensor("M2", [128, 1024], F32, kind="ExternalInput")
    A_d = nc.dram_tensor("A2", [128, 1024], F32, kind="ExternalInput")
    Bm_d = nc.dram_tensor("Bm", [32, D], F32, kind="ExternalInput")
    LiM_d = nc.dram_tensor("LiM", [8, 32], F32, kind="ExternalInput")
    LiMT_d = nc.dram_tensor("LiMT", [32, 8], F32, kind="ExternalInput")
    Wsl_d = nc.dram_tensor("Wsl_pm", [128, 256], F32, kind="ExternalInput")
    id_d = nc.dram_tensor("ident_in", [128, 128], F32, kind="ExternalInput")

    outT_d = nc.dram_tensor("outT", [D, T], F32, kind="ExternalOutput")
    x2_d = nc.dram_tensor("x2scratch", [D, T], F32, kind="Internal")
    ccb_d = nc.dram_tensor("ccbuf", [1, 64], F32, kind="Internal")

    taps = {}
    if debug:
        for nm, shp in [("t_m1", [1, T]), ("t_r1", [1, T]), ("t_xn1", [128, T]),
                        ("t_qs", [128, T]), ("t_R", [128, T]), ("t_den", [1, T]),
                        ("t_m2", [1, T]), ("t_r2", [1, T]), ("t_xn2", [128, T]),
                        ("t_z", [32, T]), ("t_hs", [8, T]), ("t_gt", [128, 64]),
                        ("t_sc", [1, 64]), ("t_zp", [32, T]), ("t_wf", [8, T]),
                        ("t_gt2", [128, 64]), ("t_zo", [32, 512]), ("t_z2", [32, T]),
                        ("t_abf", [128, T])]:
            taps[nm] = nc.dram_tensor(nm, shp, F32, kind="ExternalOutput")
        for nm in ("t_a0", "t_a1", "t_a2", "t_a3"):
            taps[nm] = nc.dram_tensor(nm, [128, T], BF16, kind="ExternalOutput")
        taps["t_ones"] = nc.dram_tensor("t_ones", [1, 128], BF16, kind="ExternalOutput")
        taps["t_scb"] = nc.dram_tensor("t_scb", [128, 4], F32, kind="ExternalOutput")
        taps["t_scb2"] = nc.dram_tensor("t_scb2", [128, 4], F32, kind="ExternalOutput")

    ctx = ExitStack()
    e = ctx.enter_context

    sems = {}
    for s in ("et", "ev", "es", "ep", "ey", "dy", "dp", "do", "cc",
              "dc", "dw0", "dw1", "dx0", "dx1", "dy0", "dy1", "do0", "do1", "dot"):
        sems[s] = e(nc.semaphore(s))

    ones1_bf = e(nc.sbuf_tensor("ones1_bf", [1, 128], BF16))
    ones128_bf = e(nc.sbuf_tensor("ones128_bf", [128, 1], BF16))
    ident = e(nc.sbuf_tensor("ident", [128, 128], F32))
    bq_pm = e(nc.sbuf_tensor("bq_pm_sb", [128, 8], F32))
    b_pm = e(nc.sbuf_tensor("b_pm_sb", [128, 32], F32))
    zT_bf = e(nc.sbuf_tensor("zT_bf", [128, 8], BF16))
    M_bf = e(nc.sbuf_tensor("M_bf", [128, 1024], BF16))
    A_bf = e(nc.sbuf_tensor("A_bf", [128, 1024], BF16))
    Bm_bf = e(nc.sbuf_tensor("Bm_bf", [32, D], BF16))
    LiM_bf = e(nc.sbuf_tensor("LiM_bf", [8, 32], BF16))
    LiMT_bf = e(nc.sbuf_tensor("LiMT_bf", [32, 8], BF16))
    Wsl_bf = e(nc.sbuf_tensor("Wsl_bf", [128, 256], BF16))

    stg = ExitStack()
    zT_f = stg.enter_context(nc.sbuf_tensor("zT_f", [128, 8], F32))
    M_f = stg.enter_context(nc.sbuf_tensor("M_f", [128, 1024], F32))
    A_f = stg.enter_context(nc.sbuf_tensor("A_f", [128, 1024], F32))
    Bm_f = stg.enter_context(nc.sbuf_tensor("Bm_f", [32, D], F32))
    LiM_f = stg.enter_context(nc.sbuf_tensor("LiM_f", [8, 32], F32))
    LiMT_f = stg.enter_context(nc.sbuf_tensor("LiMT_f", [32, 8], F32))
    Wsl_f = stg.enter_context(nc.sbuf_tensor("Wsl_f", [128, 256], F32))

    def cdma(dst_ap, src_ap):
        return kb.dma("sync", (lambda g, d=dst_ap, s=src_ap: g.dma_start(out=d, in_=s)), sem="dc")

    dl = {}
    dl["bq"] = cdma(bq_pm[:, :], bq_d[:, :])
    dl["b"] = cdma(b_pm[:, :], b_d[:, :])
    dl["zT"] = cdma(zT_f[:, :], zT_d[:, :])
    dl["M"] = cdma(M_f[:, :], M_d.ap())
    dl["A"] = cdma(A_f[:, :], A_d.ap())
    dl["Bm"] = cdma(Bm_f[:, :], Bm_d[:, :])
    dl["LiM"] = cdma(LiM_f[:, :], LiM_d[:, :])
    dl["LiMT"] = cdma(LiMT_f[:, :], LiMT_d[:, :])
    dl["Wsl"] = cdma(Wsl_f[:, :], Wsl_d[:, :])
    dl["ident"] = cdma(ident[:, :], id_d[:, :])

    all_consts = Step("sync", "dc", kb.count["dc"])
    for _k in dl:
        dl[_k] = all_consts

    c_ones1 = kb.op("vector", lambda v: v.memset(ones1_bf[:, :], 1.0))
    c_ones128 = kb.op("vector", lambda v: v.memset(ones128_bf[:, :], 1.0))
    c_id = all_consts
    c_zT = kb.op("vector", lambda v: v.tensor_copy(zT_bf[:, :], zT_f[:, :]), deps=[dl["zT"]])
    c_M = kb.op("vector", lambda v: v.tensor_copy(M_bf[:, :], M_f[:, :]), deps=[dl["M"]])
    c_A = kb.op("vector", lambda v: v.tensor_copy(A_bf[:, :], A_f[:, :]), deps=[dl["A"]])
    c_Bm = kb.op("vector", lambda v: v.tensor_scalar(
        Bm_bf[:, :], Bm_f[:, :], 0.5, None, ALU.mult), deps=[dl["Bm"]])
    c_LiM = kb.op("vector", lambda v: v.tensor_copy(LiM_bf[:, :], LiM_f[:, :]), deps=[dl["LiM"]])
    c_LiMT = kb.op("vector", lambda v: v.tensor_copy(LiMT_bf[:, :], LiMT_f[:, :]), deps=[dl["LiMT"]])
    c_Wsl = kb.op("vector", lambda v: v.tensor_copy(Wsl_bf[:, :], Wsl_f[:, :]), deps=[dl["Wsl"]])
    stg.close()
    phase0 = [c_ones1, c_ones128, c_id, c_zT, c_M, c_A, c_Bm, c_LiM, c_LiMT, c_Wsl,
              dl["bq"], dl["b"]]

    stream_bf = e(nc.sbuf_tensor("stream_bf", [128, NI * T], BF16))
    qsel = e(nc.sbuf_tensor("qsel", [128, NH * T], BF16))
    Rsb = e(nc.sbuf_tensor("Rsb", [128, NH * T], BF16))
    wring = [e(nc.sbuf_tensor(f"wring{k}", [128, 2048], F32)) for k in range(2)]
    xring = [e(nc.sbuf_tensor(f"xring{k}", [128, 2048], F32)) for k in range(2)]
    stage = [e(nc.sbuf_tensor(f"stage{k}", [128, T], F32)) for k in range(2)]
    wbf = [e(nc.sbuf_tensor(f"wbf{k}", [128, 512], BF16)) for k in range(4)]
    sqbf = [e(nc.sbuf_tensor(f"sqbf{k}", [128, T], BF16)) for k in range(4)]

    m_r = e(nc.sbuf_tensor("m_r", [1, T], F32))
    r_r = e(nc.sbuf_tensor("r_r", [1, T], F32))
    e2_r = e(nc.sbuf_tensor("e2_r", [1, T], F32))
    v_r = e(nc.sbuf_tensor("v_r", [1, T], F32))
    row_bf = [e(nc.sbuf_tensor(f"row_bf{k}", [1, T], BF16)) for k in range(2)]
    mb = e(nc.sbuf_tensor("mb", [128, T], F32))
    rb = e(nc.sbuf_tensor("rb", [128, T], F32))

    z_sb = e(nc.sbuf_tensor("z_sb", [32, T], F32))
    zodd_bf = e(nc.sbuf_tensor("zodd_bf", [32, 512], BF16))
    Hs_f = e(nc.sbuf_tensor("Hs_f", [8, T], F32))
    zp_bf = e(nc.sbuf_tensor("zp_bf", [32, T], BF16))
    wfull_bf = e(nc.sbuf_tensor("wfull_bf", [8, T], BF16))
    gt = e(nc.sbuf_tensor("gt", [128, 320], F32))
    gc = e(nc.sbuf_tensor("gc", [128, 32], F32))
    sc_sb = e(nc.sbuf_tensor("sc_sb", [1, 64], F32))
    assert nc.sbuf_bytes_remaining >= 0, nc.sbuf_bytes_remaining

    effacc = qsel[:, 0:1024]
    effbf2 = [qsel[:, 4096 + k * 512: 4096 + (k + 1) * 512] for k in range(4)]

    G_HS, G_DS, G_LG, G_EX, G_PR, G_IM, G_MK, G_W, G_WN, G_T1 = range(10)

    def gts(k):
        return gt[:, k * 32:(k + 1) * 32]

    pb = [e(nc.psum_tensor(f"pb{k}", [128, 512], F32)) for k in range(8)]

    SB = lambda i: f"sb{i}"
    QS = lambda h: f"qs{h}"
    RS = lambda h: f"R{h}"
    PB = lambda k: f"pb{k}"
    X2D = lambda o: f"x2d{o}"

    first_sync = list(phase0)

    if debug:
        kb.dma("sync", lambda g: g.dma_start(out=taps["t_a0"].ap(), in_=A_bf[:, :]),
               deps=[c_A], sem="dot")

    for gidx in range(NI // 2):
        sl = gidx % 2
        i0 = gidx * 2
        xd = kb.dma("scalar", (lambda g, i0=i0, sl=sl: g.dma_start(
            out=xring[sl].ap().rearrange("p (a c) -> p a c", a=2),
            in_=xT_d[i0 * 128:(i0 + 2) * 128, :].rearrange("(a p) c -> p a c", p=128))),
            deps=(first_sync if gidx == 0 else ()), writes=(f"xr{sl}",), sem=f"dx{sl}")
        for k in range(2):
            i = i0 + k
            cv = kb.op("vector", (lambda v, i=i, sl=sl, k=k: v.tensor_copy(
                stream_bf[:, i * T:(i + 1) * T], xring[sl][:, k * T:(k + 1) * T])),
                deps=[xd], reads=(f"xr{sl}",), writes=(SB(i),))
            sq = kb.op("scalar", (lambda s, sl=sl, k=k, i=i: s.activation(
                sqbf[i % 4][:, :], xring[sl][:, k * T:(k + 1) * T], AF.Square)),
                deps=[xd], reads=(f"xr{sl}",), writes=(f"sq{i % 4}",))

            def mk_sum(t, i=i):
                for th in range(2):
                    inst = t.matmul(pb[4 + th][0:1, :], ones128_bf[:, :],
                                    stream_bf[:, i * T + th * 512: i * T + (th + 1) * 512],
                                    start=(i == 0), stop=(i == NI - 1), skip_group_check=True)
                return inst

            def mk_sq(t, i=i):
                for th in range(2):
                    inst = t.matmul(pb[6 + th][0:1, :], ones128_bf[:, :],
                                    sqbf[i % 4][:, th * 512:(th + 1) * 512],
                                    start=(i == 0), stop=(i == NI - 1), skip_group_check=True)
                return inst

            if i == 0:
                kb.op("tensor", mk_sum, deps=[c_ones128], reads=(SB(i),),
                      writes=(PB(4), PB(5)))
                kb.op("tensor", mk_sq, reads=(f"sq{i % 4}",), writes=(PB(6), PB(7)))
            else:
                s1 = kb.op("tensor", mk_sum, reads=(SB(i),), deps=kb.rdep(PB(4)))
                s2 = kb.op("tensor", mk_sq, reads=(f"sq{i % 4}",), deps=kb.rdep(PB(6)))
                kb.did_write(s1, PB(4), PB(5))
                kb.did_write(s2, PB(6), PB(7))

    def ln_rows():
        a = kb.op("vector", lambda v: v.tensor_scalar(
            m_r[0:1, 0:512], pb[4][0:1, :], 1.0 / D, None, ALU.mult),
            reads=(PB(4),), writes=("m_r",))
        b = kb.op("vector", lambda v: v.tensor_scalar(
            m_r[0:1, 512:1024], pb[5][0:1, :], 1.0 / D, None, ALU.mult),
            reads=(PB(5),), deps=kb.wdep("m_r"))
        kb.did_write(b, "m_r")
        kb.did_read(a, "m_r")
        c = kb.op("vector", lambda v: v.tensor_scalar(
            e2_r[0:1, 0:512], pb[6][0:1, :], 1.0 / D, None, ALU.mult),
            reads=(PB(6),), writes=("e2_r",))
        d = kb.op("vector", lambda v: v.tensor_scalar(
            e2_r[0:1, 512:1024], pb[7][0:1, :], 1.0 / D, None, ALU.mult),
            reads=(PB(7),), deps=kb.wdep("e2_r"))
        kb.did_write(d, "e2_r")
        kb.did_read(c, "e2_r")
        msq = kb.op("vector", lambda v: v.tensor_tensor(
            r_r[0:1, :], m_r[0:1, :], m_r[0:1, :], ALU.mult),
            deps=[a, b], reads=("m_r",), writes=("r_r",))
        var = kb.op("vector", lambda v: v.scalar_tensor_tensor(
            v_r[0:1, :], e2_r[0:1, :], EPS_LN, r_r[0:1, :], ALU.add, ALU.subtract),
            deps=[c, d, msq], reads=("e2_r", "r_r"), writes=("v_r",))
        sd = kb.op("scalar", lambda s: s.activation(
            e2_r[0:1, :], v_r[0:1, :], AF.Sqrt),
            reads=("v_r",), writes=("e2_r",))
        rr = kb.op("vector", lambda v: v.reciprocal(r_r[0:1, :], e2_r[0:1, :]),
                   reads=("e2_r",), writes=("r_r",))
        return [a, b], rr

    def bcast_row(rowname, row, dest, destname, bank0, bank1, rbf_idx, extra_deps=()):
        cvb = kb.op("vector", (lambda v: v.tensor_copy(row_bf[rbf_idx][0:1, :], row[0:1, :])),
                    deps=list(extra_deps), reads=(rowname,), writes=(f"rbf{rbf_idx}",))
        mm0 = kb.op("tensor", (lambda t: t.matmul(
            pb[bank0][:, :], ones1_bf[:, :], row_bf[rbf_idx][0:1, 0:512],
            start=True, stop=True, skip_group_check=True)),
            deps=[c_ones1, cvb], reads=(f"rbf{rbf_idx}",), writes=(PB(bank0),))
        mm1 = kb.op("tensor", (lambda t: t.matmul(
            pb[bank1][:, :], ones1_bf[:, :], row_bf[rbf_idx][0:1, 512:1024],
            start=True, stop=True, skip_group_check=True)),
            reads=(f"rbf{rbf_idx}",), writes=(PB(bank1),))
        ev = kb.op("vector", (lambda v: [
            v.tensor_copy(dest[:, 0:512], pb[bank0][:, :]),
            v.tensor_copy(dest[:, 512:1024], pb[bank1][:, :])][-1]),
            deps=[mm0, mm1], reads=(PB(bank0), PB(bank1)), writes=(destname,))
        return ev

    mA, rA = ln_rows()
    bc_m = bcast_row("m_r", m_r, mb, "mb", 0, 1, 0, extra_deps=mA)
    bc_r = bcast_row("r_r", r_r, rb, "rb", 2, 3, 1, extra_deps=[rA])

    normA = [None] * NI
    for i in range(NI):
        sl = i % 2
        engn = "vector" if (i % 2 == 0) else "gpsimd"
        t1 = kb.op(engn, (lambda v, i=i, sl=sl: v.tensor_tensor(
            stage[sl][:, :], stream_bf[:, i * T:(i + 1) * T], mb[:, :], ALU.subtract)),
            deps=[bc_m], reads=(SB(i), "mb"), writes=(f"st{sl}",))
        t2 = kb.op(engn, (lambda v, i=i, sl=sl: v.tensor_tensor(
            stream_bf[:, i * T:(i + 1) * T], stage[sl][:, :], rb[:, :], ALU.mult)),
            deps=[bc_r, t1], reads=(f"st{sl}", "rb"), writes=(SB(i),))
        normA[i] = t2

    if debug:
        kb.dma("sync", lambda g: g.dma_start(out=taps["t_m1"].ap(), in_=m_r[0:1, :]),
               reads=("m_r",), sem="dot")
        kb.dma("sync", lambda g: g.dma_start(out=taps["t_r1"].ap(), in_=r_r[0:1, :]),
               reads=("r_r",), sem="dot")
        tc = kb.op("vector", lambda v: v.tensor_copy(stage[0][:, :], stream_bf[:, 0:T]),
                   reads=(SB(0),), writes=("st0",))
        kb.dma("sync", lambda g: g.dma_start(out=taps["t_xn1"].ap(), in_=stage[0][:, :]),
               deps=[tc], reads=("st0",), sem="dot")

    R_done = [None] * NH

    def emit_head(h, B0c):
        p2 = h % 2
        rowE = e2_r if p2 == 0 else m_r
        rowV = v_r if p2 == 0 else r_r
        rE = "e2_r" if p2 == 0 else "m_r"
        rV = "v_r" if p2 == 0 else "r_r"
        dest = mb if p2 == 0 else rb
        dn_ = "mb" if p2 == 0 else "rb"
        mmn = kb.op("tensor", (lambda t, h=h, B0=B0c: [
            t.matmul(pb[B0][:, :], M_bf[:, h * 128:(h + 1) * 128],
                     qsel[:, h * T: h * T + 512], start=True, stop=True, skip_group_check=True),
            t.matmul(pb[B0 + 1][:, :], M_bf[:, h * 128:(h + 1) * 128],
                     qsel[:, h * T + 512: h * T + 1024], start=True, stop=True, skip_group_check=True)][-1]),
            deps=[c_M], reads=(QS(h),), writes=(PB(B0c), PB(B0c + 1)))
        mmd = kb.op("tensor", (lambda t, h=h, B0=B0c: [
            t.matmul(pb[B0 + 2][0:1, :], zT_bf[:, h:h + 1],
                     qsel[:, h * T: h * T + 512], start=True, stop=True, skip_group_check=True),
            t.matmul(pb[B0 + 3][0:1, :], zT_bf[:, h:h + 1],
                     qsel[:, h * T + 512: h * T + 1024], start=True, stop=True, skip_group_check=True)][-1]),
            deps=[c_zT], reads=(QS(h),), writes=(PB(B0c + 2), PB(B0c + 3)))
        dn = kb.op("vector", (lambda v, B0=B0c, rowE=rowE: [
            v.tensor_scalar(rowE[0:1, 0:512], pb[B0 + 2][0:1, :], EPS_DEN, None, ALU.add),
            v.tensor_scalar(rowE[0:1, 512:1024], pb[B0 + 3][0:1, :], EPS_DEN, None, ALU.add)][-1]),
            deps=[mmd], reads=(PB(B0c + 2), PB(B0c + 3)), writes=(rE,))
        rec = kb.op("vector", (lambda v, rowE=rowE, rowV=rowV: v.reciprocal(
            rowV[0:1, :], rowE[0:1, :])), deps=[dn], reads=(rE,), writes=(rV,))
        ev = bcast_row(rV, rowV, dest, dn_, B0c + 2, B0c + 3, p2, extra_deps=[rec])
        rr1 = kb.op("vector", (lambda v, h=h, B0=B0c, dest=dest: v.tensor_tensor(
            Rsb[:, h * T: h * T + 512], pb[B0][:, :], dest[:, 0:512], ALU.mult)),
            deps=[ev, mmn], reads=(PB(B0c), dn_), writes=(RS(h),))
        rr2 = kb.op("vector", (lambda v, h=h, B0=B0c, dest=dest: v.tensor_tensor(
            Rsb[:, h * T + 512: h * T + 1024], pb[B0 + 1][:, :], dest[:, 512:1024], ALU.mult)),
            deps=[ev, mmn], reads=(PB(B0c + 1), dn_))
        kb.did_write(rr2, RS(h))
        kb.did_read(rr1, RS(h))
        R_done[h] = rr2

    for s_ in range(4):
        B0 = (s_ % 2) * 4
        mm_last = None
        for g in range(4):
            sl = g % 2
            wd = kb.dma("sync", (lambda y, s_=s_, g=g, sl=sl: y.dma_start(
                out=wring[sl].ap().rearrange("p (a c) -> p a c", a=8),
                in_=wq_d[s_, 8 * g:8 * g + 8, :, :].rearrange("a p c -> p a c"))),
                writes=(f"wr{sl}",), sem=f"dw{sl}")
            for k in range(8):
                i = 8 * g + k
                ws = i % 4
                wc = kb.op("scalar", (lambda s, sl=sl, k=k, ws=ws: s.activation(
                    wbf[ws][:, 0:256], wring[sl][:, k * 256:(k + 1) * 256], AF.Identity)),
                    deps=[wd], reads=(f"wr{sl}",), writes=(f"wbf{ws}",))

                def mkq(t, i=i, ws=ws, B0=B0):
                    for oc in range(2):
                        for th in range(2):
                            inst = t.matmul(
                                pb[B0 + oc * 2 + th][:, :],
                                wbf[ws][:, oc * 128:(oc + 1) * 128],
                                stream_bf[:, i * T + th * 512: i * T + (th + 1) * 512],
                                start=(i == 0), stop=(i == NI - 1), skip_group_check=True)
                    return inst

                if i == 0:
                    mm_last = kb.op("tensor", mkq, deps=[wc, normA[i]],
                                    reads=(f"wbf{ws}", SB(i)),
                                    writes=tuple(PB(B0 + kk) for kk in range(4)))
                else:
                    mm_last = kb.op("tensor", mkq, deps=[wc, normA[i]] + kb.rdep(PB(B0)),
                                    reads=(f"wbf{ws}", SB(i)))
                    kb.did_write(mm_last, *[PB(B0 + kk) for kk in range(4)])
        for oc in range(2):
            h = s_ * 2 + oc
            for th in range(2):
                bank = B0 + oc * 2 + th
                lohi = kb.op("vector", (lambda v, bank=bank, h=h: [
                    v.tensor_scalar(stage[0][:, 0:512], pb[bank][:, :],
                                    bq_pm[:, h:h + 1], 0.0, ALU.add, ALU.min),
                    v.tensor_scalar(stage[0][:, 512:1024], pb[bank][:, :],
                                    bq_pm[:, h:h + 1], 0.0, ALU.add, ALU.max)][-1]),
                    deps=[dl["bq"]], reads=(PB(bank),), writes=("st0",))
                ex = kb.op("scalar", (lambda s: s.activation(
                    stage[1][:, 0:512], stage[0][:, 0:512], AF.Exp)),
                    deps=[lohi], reads=("st0",), writes=("st1",))
                qs = kb.op("vector", (lambda v, h=h, th=th: v.tensor_tensor(
                    qsel[:, h * T + th * 512: h * T + (th + 1) * 512],
                    stage[1][:, 0:512], stage[0][:, 512:1024], ALU.add)),
                    deps=[ex], reads=("st0", "st1"), writes=(QS(h),))
                kb.did_read(qs, QS(h))
        emit_head(s_ * 2, B0)
        emit_head(s_ * 2 + 1, B0)

    if debug:
        kb.dma("sync", lambda g: g.dma_start(out=taps["t_a1"].ap(), in_=A_bf[:, :]),
               deps=kb.rdep(QS(7)), sem="dot")
        tq = kb.op("vector", lambda v: v.tensor_copy(stage[1][:, :], qsel[:, 0:T]),
                   reads=(QS(0),), writes=("st1",))
        kb.dma("sync", lambda g: g.dma_start(out=taps["t_qs"].ap(), in_=stage[1][:, :]),
               deps=[tq], reads=("st1",), sem="dot")

    if debug:
        tr = kb.op("vector", lambda v: v.tensor_copy(stage[0][:, :], Rsb[:, 0:T]),
                   reads=(RS(0),), writes=("st0",))
        kb.dma("sync", lambda g: g.dma_start(out=taps["t_R"].ap(), in_=stage[0][:, :]),
               deps=[tr], reads=("st0",), sem="dot")

    pend_stats = []

    def emit_stats(slot):
        (o, cb, sq) = slot

        def mk_sum2(t, o=o):
            for th in range(2):
                inst = t.matmul(pb[4 + th][0:1, :], ones128_bf[:, :],
                                stream_bf[:, o * T + th * 512: o * T + (th + 1) * 512],
                                start=(o == 0), stop=(o == NI - 1), skip_group_check=True)
            return inst

        def mk_sq2(t, o=o):
            for th in range(2):
                inst = t.matmul(pb[6 + th][0:1, :], ones128_bf[:, :],
                                sqbf[o % 4][:, th * 512:(th + 1) * 512],
                                start=(o == 0), stop=(o == NI - 1), skip_group_check=True)
            return inst

        if o == 0:
            kb.op("tensor", mk_sum2, deps=[cb], reads=(SB(o),), writes=(PB(4), PB(5)))
            kb.op("tensor", mk_sq2, deps=[sq], reads=(f"sq{o % 4}",), writes=(PB(6), PB(7)))
        else:
            s1 = kb.op("tensor", mk_sum2, deps=[cb] + kb.rdep(PB(4)), reads=(SB(o),))
            s2 = kb.op("tensor", mk_sq2, deps=[sq] + kb.rdep(PB(6)), reads=(f"sq{o % 4}",))
            kb.did_write(s1, PB(4), PB(5))
            kb.did_write(s2, PB(6), PB(7))

    for s_ in range(16):
        new_stats = []
        mm_o = None
        for g in range(4):
            sl = g % 2
            wd = kb.dma("sync", (lambda y, s_=s_, g=g, sl=sl: y.dma_start(
                out=wring[sl].ap().rearrange("p (a c) -> p a c", a=8),
                in_=wo_d[s_, 2 * g:2 * g + 2, :, :, :].rearrange("h j p c -> p (h j) c"))),
                writes=(f"wr{sl}",), sem=f"dw{sl}")
            eb = g % 4
            es1 = kb.op("vector", (lambda v, sl=sl: v.tensor_tensor(
                effacc.rearrange("p (h c) -> p h c", h=2),
                wring[sl].ap().rearrange("p (h u c) -> p h u c", h=2, u=2)[:, :, 0, :],
                wring[sl].ap().rearrange("p (h u c) -> p h u c", h=2, u=2)[:, :, 1, :],
                ALU.add)),
                deps=[wd], reads=(f"wr{sl}",), writes=("effacc",))
            es2 = kb.op("vector", (lambda v, eb=eb: v.tensor_tensor(
                effbf2[eb].rearrange("p (h c) -> p h c", h=2),
                effacc.rearrange("p (h u c) -> p h u c", h=2, u=2)[:, :, 0, :],
                effacc.rearrange("p (h u c) -> p h u c", h=2, u=2)[:, :, 1, :],
                ALU.add)),
                deps=[es1], reads=("effacc",), writes=(f"eff{eb}",))
            for hh in range(2):
                h = 2 * g + hh

                def mko(t, h=h, hh=hh, eb=eb):
                    for oc in range(2):
                        for th in range(2):
                            inst = t.matmul(
                                pb[oc * 2 + th][:, :],
                                effbf2[eb][:, hh * 256 + oc * 128: hh * 256 + (oc + 1) * 128],
                                Rsb[:, h * T + th * 512: h * T + (th + 1) * 512],
                                start=(h == 0), stop=(h == NH - 1), skip_group_check=True)
                    return inst

                if h == 0:
                    mm_o = kb.op("tensor", mko, deps=[es2],
                                 reads=(f"eff{eb}", RS(h)),
                                 writes=(PB(0), PB(1), PB(2), PB(3)))
                else:
                    mm_o = kb.op("tensor", mko, deps=[es2] + kb.rdep(PB(0)),
                                 reads=(f"eff{eb}", RS(h)))
                    kb.did_write(mm_o, PB(0), PB(1), PB(2), PB(3))
        o0 = s_ * 2
        slx = s_ % 2
        xd2 = kb.dma("scalar", (lambda y, o0=o0, slx=slx: y.dma_start(
            out=xring[slx].ap().rearrange("p (a c) -> p a c", a=2),
            in_=xT_d[o0 * 128:(o0 + 2) * 128, :].rearrange("(a p) c -> p a c", p=128))),
            writes=(f"xr{slx}",), sem=f"dx{slx}")
        for oc in range(2):
            o = s_ * 2 + oc
            v1 = kb.op("vector", (lambda v, oc=oc, slx=slx: [
                v.tensor_tensor(stage[oc][:, 0:512], pb[oc * 2][:, :],
                                xring[slx][:, oc * T:oc * T + 512], ALU.add),
                v.tensor_tensor(stage[oc][:, 512:1024], pb[oc * 2 + 1][:, :],
                                xring[slx][:, oc * T + 512:(oc + 1) * T], ALU.add)][-1]),
                deps=[mm_o, xd2], reads=(PB(oc * 2), PB(oc * 2 + 1), f"xr{slx}"),
                writes=(f"st{oc}",))
            kb.dma("scalar", (lambda y, o=o, oc=oc: y.dma_start(
                out=x2_d[o * 128:(o + 1) * 128, :], in_=stage[oc][:, :])),
                deps=[v1], reads=(f"st{oc}",), writes=(X2D(o),), sem=f"dy{oc}")
            cb = kb.op("gpsimd", (lambda g, o=o, oc=oc: g.tensor_copy(
                stream_bf[:, o * T:(o + 1) * T], stage[oc][:, :])),
                deps=[v1], reads=(f"st{oc}",), writes=(SB(o),))
            sq = kb.op("scalar", (lambda s, oc=oc, o=o: s.activation(
                sqbf[o % 4][:, :], stage[oc][:, :], AF.Square)),
                deps=[v1], reads=(f"st{oc}",), writes=(f"sq{o % 4}",))
            new_stats.append((o, cb, sq))
        for slot in pend_stats:
            emit_stats(slot)
        pend_stats = new_stats
    for slot in pend_stats:
        emit_stats(slot)
    x2_all_written = [Step("sync", "dy0", kb.count["dy0"]),
                      Step("sync", "dy1", kb.count["dy1"])]

    mB, rB = ln_rows()
    if debug:
        kb.dma("sync", lambda g: g.dma_start(out=taps["t_a2"].ap(), in_=A_bf[:, :]),
               deps=mB, sem="dot")
    bc_m2 = bcast_row("m_r", m_r, mb, "mb", 4, 5, 0, extra_deps=mB)
    bc_r2 = bcast_row("r_r", r_r, rb, "rb", 6, 7, 1, extra_deps=[rB])

    normB = [None] * NI
    for i in range(NI):
        sl = i % 2
        engn = "vector" if (i % 2 == 0) else "gpsimd"
        t1 = kb.op(engn, (lambda v, i=i, sl=sl: v.tensor_tensor(
            stage[sl][:, :], stream_bf[:, i * T:(i + 1) * T], mb[:, :], ALU.subtract)),
            deps=[bc_m2], reads=(SB(i), "mb"), writes=(f"st{sl}",))
        t2 = kb.op(engn, (lambda v, i=i, sl=sl: v.tensor_tensor(
            stream_bf[:, i * T:(i + 1) * T], stage[sl][:, :], rb[:, :], ALU.mult)),
            deps=[bc_r2, t1], reads=(f"st{sl}", "rb"), writes=(SB(i),))
        normB[i] = t2

    if debug:
        kb.dma("sync", lambda g: g.dma_start(out=taps["t_m2"].ap(), in_=m_r[0:1, :]),
               reads=("m_r",), sem="dot")
        kb.dma("sync", lambda g: g.dma_start(out=taps["t_r2"].ap(), in_=r_r[0:1, :]),
               reads=("r_r",), sem="dot")
        tn2 = kb.op("vector", lambda v: v.tensor_copy(stage[0][:, :], stream_bf[:, 0:T]),
                    reads=(SB(0),), writes=("st0",))
        kb.dma("sync", lambda g: g.dma_start(out=taps["t_xn2"].ap(), in_=stage[0][:, :]),
               deps=[tn2], reads=("st0",), sem="dot")

    for i in range(NI):
        def mkz(t, i=i):
            for th in range(2):
                inst = t.matmul(pb[4 + th][0:32, :], A_bf[:, i * 32:(i + 1) * 32],
                                stream_bf[:, i * T + th * 512: i * T + (th + 1) * 512],
                                start=(i == 0), stop=(i == NI - 1), skip_group_check=True)
            return inst

        if i == 0:
            kb.op("tensor", mkz, deps=[normB[i], c_A], reads=(SB(i),),
                  writes=(PB(4), PB(5)))
        else:
            sz = kb.op("tensor", mkz, deps=[normB[i]] + kb.rdep(PB(4)), reads=(SB(i),))
            kb.did_write(sz, PB(4), PB(5))
    zev = kb.op("vector", (lambda v: [
        v.tensor_copy(z_sb[:, 0:512], pb[4][0:32, :]),
        v.tensor_copy(z_sb[:, 512:1024], pb[5][0:32, :])][-1]),
        reads=(PB(4), PB(5)), writes=("z_sb",))

    for i in range(NI):
        def mkh(t, i=i):
            for th in range(2):
                inst = t.matmul(pb[6 + th][0:8, :], Wsl_bf[:, i * 8:(i + 1) * 8],
                                stream_bf[:, i * T + th * 512: i * T + (th + 1) * 512],
                                start=(i == 0), stop=(i == NI - 1), skip_group_check=True)
            return inst

        if i == 0:
            kb.op("tensor", mkh, deps=[normB[i], c_Wsl], reads=(SB(i),),
                  writes=(PB(6), PB(7)))
        else:
            shh = kb.op("tensor", mkh, deps=[normB[i]] + kb.rdep(PB(6)), reads=(SB(i),))
            kb.did_write(shh, PB(6), PB(7))
    hev = kb.op("vector", (lambda v: [
        v.tensor_scalar(Hs_f[0:8, 0:512], pb[6][0:8, :], b_pm[0:8, 0:1], None, ALU.add),
        v.tensor_scalar(Hs_f[0:8, 512:1024], pb[7][0:8, :], b_pm[0:8, 0:1], None, ALU.add)][-1]),
        deps=[dl["b"]], reads=(PB(6), PB(7)), writes=("Hs",))

    if debug:
        kb.dma("sync", lambda g: g.dma_start(out=taps["t_hs"].ap(), in_=Hs_f[:, :]),
               reads=("Hs",), sem="dot")
        kb.dma("sync", lambda g: g.dma_start(out=taps["t_z"].ap(), in_=z_sb[:, :]),
               reads=("z_sb",), sem="dot")

    zo = kb.op("vector", (lambda v: v.tensor_copy(zodd_bf[:, :], z_sb[:, 1:1024:2])),
               reads=("z_sb",), writes=("zodd",))
    for tb in range(4):
        mmds = kb.op("tensor", (lambda t, tb=tb: t.matmul(
            pb[4 + tb][:, 0:8], zodd_bf[:, tb * 128:(tb + 1) * 128], LiMT_bf[:, :],
            start=True, stop=True, skip_group_check=True)),
            deps=[zo, c_LiMT], reads=("zodd",), writes=(PB(4 + tb),))
        kb.op("vector", (lambda v, tb=tb: v.tensor_copy(
            gts(G_DS)[:, tb * 8:(tb + 1) * 8], pb[4 + tb][:, 0:8])),
            deps=[mmds], reads=(PB(4 + tb),), writes=(f"gds{tb}",))

    for tb in range(4):
        mmhT = kb.op("tensor", (lambda t, tb=tb: t.transpose(
            pb[4 + tb][:, 0:8], Hs_f[0:8, tb * 256 + 1:(tb + 1) * 256:2], ident[0:8, 0:8])),
            deps=[c_id], reads=("Hs",), writes=(PB(4 + tb),))
        kb.op("vector", (lambda v, tb=tb: v.tensor_copy(
            gts(G_HS)[:, tb * 8:(tb + 1) * 8], pb[4 + tb][:, 0:8])),
            deps=[mmhT], reads=(PB(4 + tb),), writes=(f"ghs{tb}",))

    ghs_all = tuple(f"ghs{tb}" for tb in range(4))
    gds_all = tuple(f"gds{tb}" for tb in range(4))

    if debug:
        kb.dma("sync", lambda g: g.dma_start(out=taps["t_gt2"].ap(), in_=gt[:, 0:64]),
               reads=ghs_all + gds_all, sem="dot")
        tzo = kb.op("vector", lambda v: v.tensor_copy(stage[0][0:32, 0:512], zodd_bf[:, :]),
                    reads=("zodd",), writes=("st0",))
        kb.dma("sync", lambda g: g.dma_start(out=taps["t_zo"].ap(), in_=stage[0][0:32, 0:512]),
               deps=[tzo], reads=("st0",), sem="dot")

    zrest = kb.op("vector", (lambda v: v.memset(sc_sb[0:1, 2:64], 0.0)), writes=("sc",))
    redm = []
    for (srcs, col, which) in ((ghs_all, 0, G_HS), (gds_all, 1, G_DS)):
        r1 = kb.op("vector", (lambda v, col=col, which=which: v.reduce_max(
            gc[:, col:col + 1], gts(which)[:, :], AX.X, apply_absolute_value=True)),
            reads=srcs, writes=(f"gcmax{col}",))
        mmt = kb.op("tensor", (lambda t, col=col: t.transpose(
            pb[4 + col][0:1, 0:128], gc[:, col:col + 1], ident[:, :])),
            deps=[r1, c_id], reads=(f"gcmax{col}",), writes=(PB(4 + col),))
        r2 = kb.op("vector", (lambda v, col=col: v.reduce_max(
            sc_sb[0:1, col:col + 1], pb[4 + col][0:1, 0:128], AX.X,
            apply_absolute_value=True)), deps=[mmt, zrest], reads=(PB(4 + col),))
        kb.did_write(r2, "sc")
        kb.did_read(zrest, "sc")
        redm.append(r2)

    ccw = kb.dma("gpsimd", (lambda g: g.dma_start(out=ccb_d[:, :], in_=sc_sb[0:1, :])),
                 deps=redm, reads=("sc",), sem="dp")
    cc_in = bass.AP(ccb_d.ap().tensor, 0, [[1, 1], [1, 64]])
    cc_out = bass.AP(ccb_d.ap().tensor, 0, [[1, 1], [1, 64]])
    ccr = kb.op("gpsimd", (lambda g: g.collective_compute(
        "AllReduce", ALU.max, replica_groups=[list(range(NCORES))],
        ins=[cc_in], outs=[cc_out])), deps=[ccw], sem="cc")
    ccb = kb.dma("gpsimd", (lambda g: g.dma_start(out=sc_sb[0:1, :], in_=ccb_d[:, :])),
                 deps=[ccr], writes=("sc",), sem="dp")

    post_state = {}

    def emit_post_collective():
        sc1 = kb.op("vector", (lambda v: v.tensor_scalar(
            sc_sb[0:1, 32:34], sc_sb[0:1, 0:2], EPS_DEN, None, ALU.max)),
            deps=[ccb], reads=("sc",))
        kb.did_write(sc1, "sc")
        kb.did_read(ccb, "sc")
        sc2 = kb.op("vector", (lambda v: v.reciprocal(sc_sb[0:1, 34:36], sc_sb[0:1, 32:34])),
                    deps=[sc1])
        sc3a = kb.op("vector", (lambda v: v.tensor_scalar(
            sc_sb[0:1, 36:38], sc_sb[0:1, 34:36], 0.5, None, ALU.mult)), deps=[sc2])
        sc3 = kb.op("vector", (lambda v: v.tensor_copy(
            row_bf[0][0:1, 0:2], sc_sb[0:1, 36:38])), deps=[sc3a], writes=("rbf0",))
        mmsc = kb.op("tensor", (lambda t: t.matmul(
            pb[4][:, 0:512], ones1_bf[:, :], row_bf[0][0:1, 0:512], start=True, stop=True,
            skip_group_check=True)), deps=[sc3, c_ones1], reads=("rbf0",), writes=(PB(4),))
        scev = kb.op("vector", (lambda v: v.tensor_copy(gc[:, 2:4], pb[4][:, 0:2])),
                     deps=[mmsc], reads=(PB(4),), writes=("gsc",))
        if debug:
            kb.dma("sync", lambda g: g.dma_start(out=taps["t_scb2"].ap(), in_=gc[:, 0:4]),
                   deps=[scev], reads=("gsc",), sem="dot")

        if debug:
            kb.dma("sync", lambda g: g.dma_start(out=taps["t_sc"].ap(), in_=sc_sb[0:1, :]),
                   deps=[sc3], reads=("sc",), sem="dot")

        def vop(fn, deps=(), reads=(), writes=()):
            return kb.op("vector", fn, deps, reads, writes)

        g1 = vop(lambda v: v.tensor_scalar(
            gts(G_T1)[:, :], gts(G_HS)[:, :], gc[:, 2:3], None, ALU.mult),
            reads=ghs_all + ("gsc",), writes=("gT1",))
        g2 = vop(lambda v: v.scalar_tensor_tensor(
            gts(G_LG)[:, :], gts(G_DS)[:, :], gc[:, 3:4], gts(G_T1)[:, :],
            ALU.mult, ALU.add),
            deps=[g1], reads=gds_all + ("gsc", "gT1"), writes=("gLG",))
        mx = vop(lambda v: [v.reduce_max(gc[:, 4 + tb:5 + tb],
                                         gts(G_LG)[:, tb * 8:(tb + 1) * 8], AX.X)
                            for tb in range(4)][-1],
                 deps=[g2], reads=("gLG",), writes=("gmx",))
        sh = vop(lambda v: [v.tensor_scalar(
            gts(G_T1)[:, tb * 8:(tb + 1) * 8], gts(G_LG)[:, tb * 8:(tb + 1) * 8],
            gc[:, 4 + tb:5 + tb], None, ALU.subtract) for tb in range(4)][-1],
            deps=[mx, g2], reads=("gLG", "gmx"), writes=("gT1",))
        gex = kb.op("scalar", (lambda s: s.activation(
            gts(G_EX)[:, :], gts(G_T1)[:, :], AF.Exp)), deps=[sh],
            reads=("gT1",), writes=("gEX",))
        sm = vop(lambda v: [v.reduce_sum(gc[:, 8 + tb:9 + tb],
                                         gts(G_EX)[:, tb * 8:(tb + 1) * 8], AX.X)
                            for tb in range(4)][-1],
                 deps=[gex], reads=("gEX",), writes=("gsm",))
        grc = vop(lambda v: v.reciprocal(gc[:, 12:16], gc[:, 8:12]),
                  deps=[sm], reads=("gsm",), writes=("grs",))
        pr = vop(lambda v: [v.tensor_scalar(
            gts(G_PR)[:, tb * 8:(tb + 1) * 8], gts(G_EX)[:, tb * 8:(tb + 1) * 8],
            gc[:, 12 + tb:13 + tb], None, ALU.mult) for tb in range(4)][-1],
            deps=[grc], reads=("gEX", "grs"), writes=("gPR",))
        m1 = vop(lambda v: [v.reduce_max(gc[:, 16 + tb:17 + tb],
                                         gts(G_PR)[:, tb * 8:(tb + 1) * 8], AX.X)
                            for tb in range(4)][-1],
                 deps=[pr], reads=("gPR",), writes=("gm1",))
        im = vop(lambda v: [v.tensor_scalar(
            gts(G_IM)[:, tb * 8:(tb + 1) * 8], gts(G_PR)[:, tb * 8:(tb + 1) * 8],
            gc[:, 16 + tb:17 + tb], None, ALU.is_ge) for tb in range(4)][-1],
            deps=[m1], reads=("gPR", "gm1"), writes=("gIM",))
        cn = vop(lambda v: [v.reduce_sum(gc[:, 20 + tb:21 + tb],
                                         gts(G_IM)[:, tb * 8:(tb + 1) * 8], AX.X)
                            for tb in range(4)][-1],
                 deps=[im], reads=("gIM",), writes=("gcn",))
        mk = vop(lambda v: v.scalar_tensor_tensor(
            gts(G_MK)[:, :], gts(G_IM)[:, :], -1e30, gts(G_PR)[:, :],
            ALU.mult, ALU.add),
            deps=[im], reads=("gIM", "gPR"), writes=("gMK",))
        m2 = vop(lambda v: [v.reduce_max(gc[:, 24 + tb:25 + tb],
                                         gts(G_MK)[:, tb * 8:(tb + 1) * 8], AX.X)
                            for tb in range(4)][-1],
                 deps=[mk], reads=("gMK",), writes=("gm2",))
        fl = vop(lambda v: v.tensor_scalar(gc[:, 28:32], gc[:, 20:24], 1.5, None, ALU.is_ge),
                 deps=[cn], reads=("gcn",), writes=("gfl",))
        d12 = vop(lambda v: v.tensor_tensor(gc[:, 16:20], gc[:, 16:20], gc[:, 24:28], ALU.subtract),
                  deps=[m2], reads=("gm1", "gm2"), writes=("gm1",))
        fd = vop(lambda v: v.tensor_tensor(gc[:, 28:32], gc[:, 28:32], gc[:, 16:20], ALU.mult),
                 deps=[fl, d12], reads=("gfl", "gm1"), writes=("gfl",))
        thr = vop(lambda v: v.tensor_tensor(gc[:, 24:28], gc[:, 24:28], gc[:, 28:32], ALU.add),
                  deps=[fd], reads=("gm2", "gfl"), writes=("gm2",))
        pm = vop(lambda v: [v.tensor_scalar(
            gts(G_T1)[:, tb * 8:(tb + 1) * 8], gts(G_PR)[:, tb * 8:(tb + 1) * 8],
            gc[:, 24 + tb:25 + tb], 2.0, ALU.subtract, ALU.mult) for tb in range(4)][-1],
            deps=[thr, gex], reads=("gPR", "gm2"), writes=("gT1",))
        sg = kb.op("scalar", (lambda s: s.activation(
            gts(G_MK)[:, :], gts(G_T1)[:, :], AF.Sigmoid)),
            deps=[pm, m2], reads=("gT1",), writes=("gMK",))
        w_ = vop(lambda v: v.tensor_tensor(gts(G_W)[:, :], gts(G_PR)[:, :], gts(G_MK)[:, :], ALU.mult),
                 deps=[sg], reads=("gPR", "gMK"), writes=("gW",))
        ws = vop(lambda v: [v.reduce_sum(gc[:, 16 + tb:17 + tb],
                                         gts(G_W)[:, tb * 8:(tb + 1) * 8], AX.X)
                            for tb in range(4)][-1],
                 deps=[w_, d12, fd], reads=("gW",), writes=("gm1",))
        wsp = vop(lambda v: v.tensor_scalar(gc[:, 20:24], gc[:, 16:20], 1e-9, None, ALU.add),
                  deps=[ws], reads=("gm1",), writes=("gcn",))
        wrc = vop(lambda v: v.reciprocal(gc[:, 24:28], gc[:, 20:24]),
                  deps=[wsp, thr], reads=("gcn",), writes=("gm2",))
        wn = vop(lambda v: [v.tensor_scalar(
            gts(G_WN)[:, tb * 8:(tb + 1) * 8], gts(G_W)[:, tb * 8:(tb + 1) * 8],
            gc[:, 24 + tb:25 + tb], None, ALU.mult) for tb in range(4)][-1],
            deps=[wrc], reads=("gW", "gm2"), writes=("gWN",))

        if debug:
            tg = vop(lambda v: [
                v.tensor_copy(gt[:, 64:96], gts(G_PR)[:, :]),
                v.tensor_copy(gt[:, 96:128], gts(G_WN)[:, :])][-1],
                deps=[wn], reads=("gPR", "gWN"), writes=("gLG", "gEX"))
            kb.dma("sync", lambda g: g.dma_start(out=taps["t_gt"].ap(), in_=gt[:, 64:128]),
                   deps=[tg], sem="dot")

        wf_last = None
        for tb in range(4):
            mmw = kb.op("tensor", (lambda t, tb=tb: t.transpose(
                pb[5][0:8, 0:128], gts(G_WN)[:, tb * 8:(tb + 1) * 8], ident[:, :])),
                deps=[wn, c_id], reads=("gWN",), writes=(PB(5),))
            wf_last = kb.op("vector", (lambda v, tb=tb: [
                v.tensor_copy(wfull_bf[0:8, tb * 256 + 0:(tb + 1) * 256:2], pb[5][0:8, 0:128]),
                v.tensor_copy(wfull_bf[0:8, tb * 256 + 1:(tb + 1) * 256:2], pb[5][0:8, 0:128])][-1]),
                deps=[mmw], reads=(PB(5),))
            if tb == 0:
                kb.did_write(wf_last, "wfull")
            else:
                kb.did_read(wf_last, "wfull")

        mmp = kb.op("tensor", (lambda t: [
            t.matmul(pb[4][0:32, :], LiM_bf[:, :], wfull_bf[0:8, 0:512],
                     start=True, stop=True, skip_group_check=True),
            t.matmul(pb[5][0:32, :], LiM_bf[:, :], wfull_bf[0:8, 512:1024],
                     start=True, stop=True, skip_group_check=True)][-1]),
            deps=[c_LiM] + kb.bufs["wfull"]["pend"] + [kb.bufs["wfull"]["w"]],
            writes=(PB(4), PB(5)))
        zp_done = kb.op("vector", (lambda v: [
            v.tensor_tensor(zp_bf[:, 0:512], z_sb[:, 0:512], pb[4][0:32, :], ALU.mult),
            v.tensor_tensor(zp_bf[:, 512:1024], z_sb[:, 512:1024], pb[5][0:32, :], ALU.mult)][-1]),
            deps=[mmp], reads=(PB(4), PB(5), "z_sb"), writes=("zp",))

        if debug:
            tzp = vop(lambda v: v.tensor_copy(stage[0][0:32, :], zp_bf[:, :]),
                      reads=("zp",), writes=("st0",))
            kb.dma("sync", lambda g: g.dma_start(out=taps["t_zp"].ap(), in_=stage[0][0:32, :]),
                   deps=[tzp], reads=("st0",), sem="dot")
            twf = vop(lambda v: v.tensor_copy(stage[1][0:8, :], wfull_bf[:, :]),
                      deps=[wf_last], writes=("st1",))
            kb.dma("sync", lambda g: g.dma_start(out=taps["t_wf"].ap(), in_=stage[1][0:8, :]),
                   deps=[twf], reads=("st1",), sem="dot")


        post_state["zp_done"] = zp_done

    out_steps = []
    for s_ in range(16):
        B0 = 0 if (s_ < 2 or s_ % 2 == 0) else 4
        mm_f = None
        for g in range(4):
            sl = g % 2
            wd = kb.dma("sync", (lambda y, s_=s_, g=g, sl=sl: y.dma_start(
                out=wring[sl].ap().rearrange("p (a c) -> p a c", a=8),
                in_=wt_d[s_, 8 * g:8 * g + 8, :, :].rearrange("a p c -> p a c"))),
                writes=(f"wr{sl}",), sem=f"dw{sl}")
            for k in range(8):
                i = 8 * g + k
                ws = i % 4
                wc = kb.op("scalar", (lambda s, sl=sl, k=k, ws=ws: s.activation(
                    wbf[ws][:, 0:256], wring[sl][:, k * 256:(k + 1) * 256], AF.Identity)),
                    deps=[wd], reads=(f"wr{sl}",), writes=(f"wbf{ws}",))

                def mkf(t, i=i, ws=ws, B0=B0):
                    for oc in range(2):
                        for th in range(2):
                            inst = t.matmul(
                                pb[B0 + oc * 2 + th][:, :],
                                wbf[ws][:, oc * 128:(oc + 1) * 128],
                                stream_bf[:, i * T + th * 512: i * T + (th + 1) * 512],
                                start=(i == 0), stop=False, skip_group_check=True)
                    return inst

                if i == 0:
                    mm_f = kb.op("tensor", mkf, deps=[wc, normB[i]],
                                 reads=(f"wbf{ws}", SB(i)),
                                 writes=tuple(PB(B0 + kk) for kk in range(4)))
                else:
                    mm_f = kb.op("tensor", mkf, deps=[wc, normB[i]] + kb.rdep(PB(B0)),
                                 reads=(f"wbf{ws}", SB(i)))
                    kb.did_write(mm_f, *[PB(B0 + kk) for kk in range(4)])

        if s_ == 0:
            emit_post_collective()

        def mkdelta(t, s_=s_, B0=B0):
            for oc in range(2):
                for th in range(2):
                    inst = t.matmul(
                        pb[B0 + oc * 2 + th][:, :],
                        Bm_bf[:, (s_ * 2 + oc) * 128:(s_ * 2 + oc + 1) * 128],
                        zp_bf[:, th * 512:(th + 1) * 512],
                        start=False, stop=True, skip_group_check=True)
            return inst

        mm_d = kb.op("tensor", mkdelta, deps=[mm_f, post_state["zp_done"], c_Bm], reads=("zp",))
        kb.did_write(mm_d, *[PB(B0 + kk) for kk in range(4)])

        o0 = s_ * 2
        slx = s_ % 2
        x2rd = kb.dma("scalar", (lambda y, o0=o0, slx=slx: y.dma_start(
            out=xring[slx].ap().rearrange("p (a c) -> p a c", a=2),
            in_=x2_d[o0 * 128:(o0 + 2) * 128, :].rearrange("(a p) c -> p a c", p=128))),
            deps=[x2_all_written], writes=(f"xr{slx}",), sem=f"dx{slx}")
        for oc in range(2):
            o = s_ * 2 + oc
            v2 = kb.op("vector", (lambda v, o=o, oc=oc, slx=slx, B0=B0: [
                v.scalar_tensor_tensor(
                    stage[oc][:, 0:512], pb[B0 + oc * 2][:, :], b_pm[:, o:o + 1],
                    xring[slx][:, oc * T:oc * T + 512], ALU.add, ALU.add),
                v.scalar_tensor_tensor(
                    stage[oc][:, 512:1024], pb[B0 + oc * 2 + 1][:, :], b_pm[:, o:o + 1],
                    xring[slx][:, oc * T + 512:(oc + 1) * T], ALU.add, ALU.add)][-1]),
                deps=[mm_d, x2rd, dl["b"]],
                reads=(PB(B0 + oc * 2), PB(B0 + oc * 2 + 1), f"xr{slx}"),
                writes=(f"st{oc}",))
            ow = kb.dma("scalar", (lambda y, o=o, oc=oc: y.dma_start(
                out=outT_d[o * 128:(o + 1) * 128, :], in_=stage[oc][:, :])),
                deps=[v2], reads=(f"st{oc}",), sem=f"do{oc}")
            out_steps.append(ow)

    if debug:
        tz2 = kb.op("vector", lambda v: v.tensor_copy(stage[0][0:32, :], z_sb[:, :]),
                    deps=out_steps, reads=("z_sb",), writes=("st0",))
        kb.dma("sync", lambda g: g.dma_start(out=taps["t_z2"].ap(), in_=stage[0][0:32, :]),
               deps=[tz2], reads=("st0",), sem="dot")
        tabf = kb.op("vector", lambda v: v.tensor_copy(stage[1][:, :], A_bf[:, :]),
                     deps=out_steps, writes=("st1",))
        kb.dma("sync", lambda g: g.dma_start(out=taps["t_abf"].ap(), in_=stage[1][:, :]),
               deps=[tabf], reads=("st1",), sem="dot")
        kb.dma("sync", lambda g: g.dma_start(out=taps["t_a3"].ap(), in_=A_bf[:, :]),
               deps=out_steps, sem="dot")
        kb.dma("sync", lambda g: g.dma_start(out=taps["t_ones"].ap(), in_=ones1_bf[:, :]),
               deps=out_steps, sem="dot")
        kb.dma("sync", lambda g: g.dma_start(out=taps["t_scb"].ap(), in_=gc[:, 0:4]),
               deps=out_steps, reads=("gsc",), sem="dot")

    kb.op("sync", (lambda y: None), deps=out_steps)

    kb.emit(nc, sems)
    ctx.close()
    return nc


_SEL = np.concatenate([np.arange(g * 128, (g + 1) * 128) for g in range(0, 32, 4)])


def _stage_inputs(L0_attn_hidden_states, L0_attn_Wq, L0_attn_bq, L0_attn_Wo,
                  L0_attn_M, L0_attn_z, L0_ffn_W, L0_ffn_b, L0_ffn_A,
                  L0_ffn_Bmat, L0_ffn_LiMEs):
    x = np.asarray(L0_attn_hidden_states, dtype=np.float32).reshape(8192, D)
    xT = np.ascontiguousarray(x.T)
    Wq = np.asarray(L0_attn_Wq, np.float32)
    WqTsel = np.ascontiguousarray(Wq[_SEL, :].T)
    wq_st = np.ascontiguousarray(WqTsel.reshape(NI, 128, 4, 256).transpose(2, 0, 1, 3))
    bq = np.asarray(L0_attn_bq, np.float32)
    bq_pm = np.ascontiguousarray(bq[_SEL].reshape(8, 128).T)
    WoT = np.ascontiguousarray(np.asarray(L0_attn_Wo, np.float32).T)
    wo_st = np.ascontiguousarray(WoT.reshape(8, 4, 128, 16, 256).transpose(3, 0, 1, 2, 4)[:, :, [0, 2, 1, 3]])
    WT = np.ascontiguousarray(np.asarray(L0_ffn_W, np.float32).T)
    wt_st = np.ascontiguousarray(WT.reshape(NI, 128, 16, 256).transpose(2, 0, 1, 3))
    b_pm = np.ascontiguousarray(np.asarray(L0_ffn_b, np.float32).reshape(32, 128).T)
    wsl_pm = np.ascontiguousarray(
        WT[:, 0:8].reshape(NI, 128, 8).transpose(1, 0, 2).reshape(128, 256))
    zT = np.ascontiguousarray(np.asarray(L0_attn_z, np.float32).T)
    M2 = np.ascontiguousarray(
        np.asarray(L0_attn_M, np.float32).transpose(1, 0, 2).reshape(128, 1024))
    A2 = np.ascontiguousarray(
        np.asarray(L0_ffn_A, np.float32).reshape(32, 128, 32).transpose(1, 0, 2).reshape(128, 1024))
    Bm = np.ascontiguousarray(np.asarray(L0_ffn_Bmat, np.float32))
    LiM = np.ascontiguousarray(np.asarray(L0_ffn_LiMEs, np.float32))
    LiMT = np.ascontiguousarray(LiM.T)
    shared = {"wq_st": wq_st, "wo_st": wo_st, "wt_st": wt_st, "bq_pm": bq_pm,
              "b_pm": b_pm, "zT": zT, "M2": M2, "A2": A2, "Bm": Bm,
              "LiM": LiM, "LiMT": LiMT, "Wsl_pm": wsl_pm,
              "ident_in": np.eye(128, dtype=np.float32)}
    in_maps = []
    for c in range(NCORES):
        m = dict(shared)
        m["xT"] = np.ascontiguousarray(xT[:, c * T:(c + 1) * T])
        in_maps.append(m)
    return in_maps


def kernel(L0_attn_hidden_states, L0_attn_Wq, L0_attn_bq, L0_attn_Wk,
           L0_attn_bk, L0_attn_Wv, L0_attn_bv, L0_attn_Wo, L0_attn_M,
           L0_attn_z, L0_ffn_W, L0_ffn_b, L0_ffn_A, L0_ffn_Bmat, L0_ffn_LiMEs):
    in_maps = _stage_inputs(L0_attn_hidden_states, L0_attn_Wq, L0_attn_bq,
                            L0_attn_Wo, L0_attn_M, L0_attn_z, L0_ffn_W,
                            L0_ffn_b, L0_ffn_A, L0_ffn_Bmat, L0_ffn_LiMEs)
    nc = build(debug=False)
    res = run_bass_kernel_spmd(nc, in_maps, core_ids=list(range(NCORES)))
    parts = [np.ascontiguousarray(res.results[c]["outT"].T) for c in range(NCORES)]
    out = np.concatenate(parts, axis=0).reshape(2, 4096, 4096)
    return out


# version 11
# speedup vs baseline: 1.0655x; 1.0655x over previous
import numpy as np
from contextlib import ExitStack

import concourse.bass as bass
import concourse.mybir as mybir
from concourse.bass_utils import run_bass_kernel_spmd

F32 = mybir.dt.float32
BF16 = mybir.dt.bfloat16
AF = mybir.ActivationFunctionType
ALU = mybir.AluOpType
AX = mybir.AxisListType

D = 4096
T = 1024
NI = D // 128
NH = 8
NCORES = 8
EPS_LN = 1e-5
EPS_DEN = 1e-6


class Step:
    __slots__ = ("eng", "sem", "val")

    def __init__(self, eng, sem, val):
        self.eng, self.sem, self.val = eng, sem, val


class KB:

    ENGS = ("sync", "tensor", "vector", "scalar", "gpsimd")

    def __init__(self):
        self.sched = []
        self.count = {}
        self.bufs = {}

    def _b(self, name):
        if name not in self.bufs:
            self.bufs[name] = {"w": None, "pend": []}
        return self.bufs[name]

    def rdep(self, *names):
        return [self._b(n)["w"] for n in names]

    def wdep(self, *names):
        out = []
        for n in names:
            b = self._b(n)
            out += b["pend"]
            out.append(b["w"])
        return out

    def did_read(self, step, *names):
        for n in names:
            self._b(n)["pend"].append(step)

    def did_write(self, step, *names):
        for n in names:
            b = self._b(n)
            b["w"] = step
            b["pend"] = []

    def op(self, eng, fn, deps=(), reads=(), writes=(), sem=None, amt=1):
        if sem is None:
            sem = {"tensor": "et", "vector": "ev", "scalar": "es",
                   "gpsimd": "ep", "sync": "ey"}[eng]
        alldeps = list(deps) + self.rdep(*reads) + self.wdep(*writes)
        waits = []
        for d in alldeps:
            if d is None:
                continue
            if isinstance(d, (list, tuple)):
                waits += [(x.sem, x.val) for x in d if x is not None]
            else:
                waits.append((d.sem, d.val))
        self.count[sem] = self.count.get(sem, 0) + amt
        st = Step(eng, sem, self.count[sem])
        self.sched.append((eng, waits, fn, sem, amt))
        self.did_read(st, *reads)
        self.did_write(st, *writes)
        return st

    def dma(self, eng, fn, deps=(), reads=(), writes=(), sem="dy"):
        return self.op(eng, fn, deps, reads, writes, sem=sem, amt=16)

    def emit(self, nc, sems):
        with nc.Block() as block:
            for eng_name in self.ENGS:
                entries = [e for e in self.sched if e[0] == eng_name]
                if not entries:
                    continue

                def section(eng, entries=entries):
                    waited = {}
                    for (_, waits, fn, sem, amt) in entries:
                        best = {}
                        for (s, v) in waits:
                            if v > best.get(s, -1):
                                best[s] = v
                        for s, v in sorted(best.items()):
                            if waited.get(s, 0) < v:
                                eng.wait_ge(sems[s], v)
                                waited[s] = v
                        inst = fn(eng)
                        if inst is not None:
                            inst.then_inc(sems[sem], amt)

                getattr(block, eng_name)(section)


def build(debug=False):
    nc = bass.Bass("TRN2", target_bir_lowering=False, debug=False, num_devices=8)
    kb = KB()

    xT_d = nc.dram_tensor("xT", [D, T], F32, kind="ExternalInput")
    wq_d = nc.dram_tensor("wq_st", [4, NI, 128, 256], F32, kind="ExternalInput")
    wo_d = nc.dram_tensor("wo_st", [16, 8, 4, 128, 256], F32, kind="ExternalInput")
    wt_d = nc.dram_tensor("wt_st", [16, NI, 128, 256], F32, kind="ExternalInput")
    bq_d = nc.dram_tensor("bq_pm", [128, 8], F32, kind="ExternalInput")
    b_d = nc.dram_tensor("b_pm", [128, 32], F32, kind="ExternalInput")
    zT_d = nc.dram_tensor("zT", [128, 8], F32, kind="ExternalInput")
    M_d = nc.dram_tensor("M2", [128, 1024], F32, kind="ExternalInput")
    A_d = nc.dram_tensor("A2", [128, 1024], F32, kind="ExternalInput")
    Bm_d = nc.dram_tensor("Bm", [32, D], F32, kind="ExternalInput")
    LiM_d = nc.dram_tensor("LiM", [8, 32], F32, kind="ExternalInput")
    LiMT_d = nc.dram_tensor("LiMT", [32, 8], F32, kind="ExternalInput")
    Wsl_d = nc.dram_tensor("Wsl_pm", [128, 256], F32, kind="ExternalInput")
    id_d = nc.dram_tensor("ident_in", [128, 128], F32, kind="ExternalInput")

    outT_d = nc.dram_tensor("outT", [D, T], F32, kind="ExternalOutput")
    x2_d = nc.dram_tensor("x2scratch", [D, T], F32, kind="Internal")
    ccb_d = nc.dram_tensor("ccbuf", [1, 64], F32, kind="Internal")

    taps = {}
    if debug:
        for nm, shp in [("t_m1", [1, T]), ("t_r1", [1, T]), ("t_xn1", [128, T]),
                        ("t_qs", [128, T]), ("t_R", [128, T]), ("t_den", [1, T]),
                        ("t_m2", [1, T]), ("t_r2", [1, T]), ("t_xn2", [128, T]),
                        ("t_z", [32, T]), ("t_hs", [8, T]), ("t_gt", [128, 64]),
                        ("t_sc", [1, 64]), ("t_zp", [32, T]), ("t_wf", [8, T]),
                        ("t_gt2", [128, 64]), ("t_zo", [32, 512]), ("t_z2", [32, T]),
                        ("t_abf", [128, T])]:
            taps[nm] = nc.dram_tensor(nm, shp, F32, kind="ExternalOutput")
        for nm in ("t_a0", "t_a1", "t_a2", "t_a3"):
            taps[nm] = nc.dram_tensor(nm, [128, T], BF16, kind="ExternalOutput")
        taps["t_ones"] = nc.dram_tensor("t_ones", [1, 128], BF16, kind="ExternalOutput")
        taps["t_scb"] = nc.dram_tensor("t_scb", [128, 4], F32, kind="ExternalOutput")
        taps["t_scb2"] = nc.dram_tensor("t_scb2", [128, 4], F32, kind="ExternalOutput")

    ctx = ExitStack()
    e = ctx.enter_context

    sems = {}
    for s in ("et", "ev", "es", "ep", "ey", "dy", "dp", "do", "cc",
              "dc", "dw0", "dw1", "dx0", "dx1", "dy0", "dy1", "do0", "do1", "dot"):
        sems[s] = e(nc.semaphore(s))

    ones1_bf = e(nc.sbuf_tensor("ones1_bf", [1, 128], BF16))
    ones128_bf = e(nc.sbuf_tensor("ones128_bf", [128, 1], BF16))
    ident = e(nc.sbuf_tensor("ident", [128, 128], F32))
    bq_pm = e(nc.sbuf_tensor("bq_pm_sb", [128, 8], F32))
    b_pm = e(nc.sbuf_tensor("b_pm_sb", [128, 32], F32))
    zT_bf = e(nc.sbuf_tensor("zT_bf", [128, 8], BF16))
    M_bf = e(nc.sbuf_tensor("M_bf", [128, 1024], BF16))
    A_bf = e(nc.sbuf_tensor("A_bf", [128, 1024], BF16))
    Bm_bf = e(nc.sbuf_tensor("Bm_bf", [32, D], BF16))
    LiM_bf = e(nc.sbuf_tensor("LiM_bf", [8, 32], BF16))
    LiMT_bf = e(nc.sbuf_tensor("LiMT_bf", [32, 8], BF16))
    Wsl_bf = e(nc.sbuf_tensor("Wsl_bf", [128, 256], BF16))

    stg = ExitStack()
    zT_f = stg.enter_context(nc.sbuf_tensor("zT_f", [128, 8], F32))
    M_f = stg.enter_context(nc.sbuf_tensor("M_f", [128, 1024], F32))
    A_f = stg.enter_context(nc.sbuf_tensor("A_f", [128, 1024], F32))
    Bm_f = stg.enter_context(nc.sbuf_tensor("Bm_f", [32, D], F32))
    LiM_f = stg.enter_context(nc.sbuf_tensor("LiM_f", [8, 32], F32))
    LiMT_f = stg.enter_context(nc.sbuf_tensor("LiMT_f", [32, 8], F32))
    Wsl_f = stg.enter_context(nc.sbuf_tensor("Wsl_f", [128, 256], F32))

    def cdma(dst_ap, src_ap):
        return kb.dma("sync", (lambda g, d=dst_ap, s=src_ap: g.dma_start(out=d, in_=s)), sem="dc")

    dl = {}
    dl["bq"] = cdma(bq_pm[:, :], bq_d[:, :])
    dl["b"] = cdma(b_pm[:, :], b_d[:, :])
    dl["zT"] = cdma(zT_f[:, :], zT_d[:, :])
    dl["M"] = cdma(M_f[:, :], M_d.ap())
    dl["A"] = cdma(A_f[:, :], A_d.ap())
    dl["Bm"] = cdma(Bm_f[:, :], Bm_d[:, :])
    dl["LiM"] = cdma(LiM_f[:, :], LiM_d[:, :])
    dl["LiMT"] = cdma(LiMT_f[:, :], LiMT_d[:, :])
    dl["Wsl"] = cdma(Wsl_f[:, :], Wsl_d[:, :])
    dl["ident"] = cdma(ident[:, :], id_d[:, :])

    all_consts = Step("sync", "dc", kb.count["dc"])
    for _k in dl:
        dl[_k] = all_consts

    c_ones1 = kb.op("vector", lambda v: v.memset(ones1_bf[:, :], 1.0))
    c_ones128 = kb.op("vector", lambda v: v.memset(ones128_bf[:, :], 1.0))
    c_id = all_consts
    c_zT = kb.op("vector", lambda v: v.tensor_copy(zT_bf[:, :], zT_f[:, :]), deps=[dl["zT"]])
    c_M = kb.op("vector", lambda v: v.tensor_copy(M_bf[:, :], M_f[:, :]), deps=[dl["M"]])
    c_A = kb.op("vector", lambda v: v.tensor_copy(A_bf[:, :], A_f[:, :]), deps=[dl["A"]])
    c_Bm = kb.op("vector", lambda v: v.tensor_scalar(
        Bm_bf[:, :], Bm_f[:, :], 0.5, None, ALU.mult), deps=[dl["Bm"]])
    c_LiM = kb.op("vector", lambda v: v.tensor_copy(LiM_bf[:, :], LiM_f[:, :]), deps=[dl["LiM"]])
    c_LiMT = kb.op("vector", lambda v: v.tensor_copy(LiMT_bf[:, :], LiMT_f[:, :]), deps=[dl["LiMT"]])
    c_Wsl = kb.op("vector", lambda v: v.tensor_copy(Wsl_bf[:, :], Wsl_f[:, :]), deps=[dl["Wsl"]])
    stg.close()
    phase0 = [c_ones1, c_ones128, c_id, c_zT, c_M, c_A, c_Bm, c_LiM, c_LiMT, c_Wsl,
              dl["bq"], dl["b"]]

    stream_bf = e(nc.sbuf_tensor("stream_bf", [128, NI * T], BF16))
    qsel = e(nc.sbuf_tensor("qsel", [128, NH * T], BF16))
    Rsb = e(nc.sbuf_tensor("Rsb", [128, NH * T], BF16))
    wring = [e(nc.sbuf_tensor(f"wring{k}", [128, 2048], F32)) for k in range(2)]
    xring = [e(nc.sbuf_tensor(f"xring{k}", [128, 2048], F32)) for k in range(2)]
    stage = [e(nc.sbuf_tensor(f"stage{k}", [128, T], F32)) for k in range(2)]
    wbf = [e(nc.sbuf_tensor(f"wbf{k}", [128, 512], BF16)) for k in range(4)]
    sqbf = [e(nc.sbuf_tensor(f"sqbf{k}", [128, T], BF16)) for k in range(4)]

    m_r = e(nc.sbuf_tensor("m_r", [1, T], F32))
    r_r = e(nc.sbuf_tensor("r_r", [1, T], F32))
    e2_r = e(nc.sbuf_tensor("e2_r", [1, T], F32))
    v_r = e(nc.sbuf_tensor("v_r", [1, T], F32))
    row_bf = [e(nc.sbuf_tensor(f"row_bf{k}", [1, T], BF16)) for k in range(2)]
    mb = e(nc.sbuf_tensor("mb", [128, T], F32))
    rb = e(nc.sbuf_tensor("rb", [128, T], F32))

    z_sb = e(nc.sbuf_tensor("z_sb", [32, T], F32))
    zodd_bf = e(nc.sbuf_tensor("zodd_bf", [32, 512], BF16))
    Hs_f = e(nc.sbuf_tensor("Hs_f", [8, T], F32))
    zp_bf = e(nc.sbuf_tensor("zp_bf", [32, T], BF16))
    wfull_bf = e(nc.sbuf_tensor("wfull_bf", [8, T], BF16))
    gt = e(nc.sbuf_tensor("gt", [128, 320], F32))
    gc = e(nc.sbuf_tensor("gc", [128, 32], F32))
    sc_sb = e(nc.sbuf_tensor("sc_sb", [1, 64], F32))
    assert nc.sbuf_bytes_remaining >= 0, nc.sbuf_bytes_remaining

    effacc = qsel[:, 0:1024]
    effbf2 = [qsel[:, 4096 + k * 512: 4096 + (k + 1) * 512] for k in range(4)]

    G_HS, G_DS, G_LG, G_EX, G_PR, G_IM, G_MK, G_W, G_WN, G_T1 = range(10)

    def gts(k):
        return gt[:, k * 32:(k + 1) * 32]

    pb = [e(nc.psum_tensor(f"pb{k}", [128, 512], F32)) for k in range(8)]

    SB = lambda i: f"sb{i}"
    QS = lambda h: f"qs{h}"
    RS = lambda h: f"R{h}"
    PB = lambda k: f"pb{k}"
    X2D = lambda o: f"x2d{o}"

    first_sync = list(phase0)

    if debug:
        kb.dma("sync", lambda g: g.dma_start(out=taps["t_a0"].ap(), in_=A_bf[:, :]),
               deps=[c_A], sem="dot")

    for gidx in range(NI // 2):
        sl = gidx % 2
        i0 = gidx * 2
        xd = kb.dma("scalar", (lambda g, i0=i0, sl=sl: g.dma_start(
            out=xring[sl].ap().rearrange("p (a c) -> p a c", a=2),
            in_=xT_d[i0 * 128:(i0 + 2) * 128, :].rearrange("(a p) c -> p a c", p=128))),
            deps=(first_sync if gidx == 0 else ()), writes=(f"xr{sl}",), sem=f"dx{sl}")
        for k in range(2):
            i = i0 + k
            cv = kb.op("vector", (lambda v, i=i, sl=sl, k=k: v.tensor_copy(
                stream_bf[:, i * T:(i + 1) * T], xring[sl][:, k * T:(k + 1) * T])),
                deps=[xd], reads=(f"xr{sl}",), writes=(SB(i),))
            sq = kb.op("scalar", (lambda s, sl=sl, k=k, i=i: s.activation(
                sqbf[i % 4][:, :], xring[sl][:, k * T:(k + 1) * T], AF.Square)),
                deps=[xd], reads=(f"xr{sl}",), writes=(f"sq{i % 4}",))

            def mk_sum(t, i=i):
                for th in range(2):
                    inst = t.matmul(pb[4 + th][0:1, :], ones128_bf[:, :],
                                    stream_bf[:, i * T + th * 512: i * T + (th + 1) * 512],
                                    start=(i == 0), stop=(i == NI - 1), skip_group_check=True)
                return inst

            def mk_sq(t, i=i):
                for th in range(2):
                    inst = t.matmul(pb[6 + th][0:1, :], ones128_bf[:, :],
                                    sqbf[i % 4][:, th * 512:(th + 1) * 512],
                                    start=(i == 0), stop=(i == NI - 1), skip_group_check=True)
                return inst

            if i == 0:
                kb.op("tensor", mk_sum, deps=[c_ones128], reads=(SB(i),),
                      writes=(PB(4), PB(5)))
                kb.op("tensor", mk_sq, reads=(f"sq{i % 4}",), writes=(PB(6), PB(7)))
            else:
                s1 = kb.op("tensor", mk_sum, reads=(SB(i),), deps=kb.rdep(PB(4)))
                s2 = kb.op("tensor", mk_sq, reads=(f"sq{i % 4}",), deps=kb.rdep(PB(6)))
                kb.did_write(s1, PB(4), PB(5))
                kb.did_write(s2, PB(6), PB(7))

    def ln_rows():
        a = kb.op("vector", lambda v: v.tensor_scalar(
            m_r[0:1, 0:512], pb[4][0:1, :], 1.0 / D, None, ALU.mult),
            reads=(PB(4),), writes=("m_r",))
        b = kb.op("vector", lambda v: v.tensor_scalar(
            m_r[0:1, 512:1024], pb[5][0:1, :], 1.0 / D, None, ALU.mult),
            reads=(PB(5),), deps=kb.wdep("m_r"))
        kb.did_write(b, "m_r")
        kb.did_read(a, "m_r")
        c = kb.op("vector", lambda v: v.tensor_scalar(
            e2_r[0:1, 0:512], pb[6][0:1, :], 1.0 / D, None, ALU.mult),
            reads=(PB(6),), writes=("e2_r",))
        d = kb.op("vector", lambda v: v.tensor_scalar(
            e2_r[0:1, 512:1024], pb[7][0:1, :], 1.0 / D, None, ALU.mult),
            reads=(PB(7),), deps=kb.wdep("e2_r"))
        kb.did_write(d, "e2_r")
        kb.did_read(c, "e2_r")
        msq = kb.op("vector", lambda v: v.tensor_tensor(
            r_r[0:1, :], m_r[0:1, :], m_r[0:1, :], ALU.mult),
            deps=[a, b], reads=("m_r",), writes=("r_r",))
        var = kb.op("vector", lambda v: v.scalar_tensor_tensor(
            v_r[0:1, :], e2_r[0:1, :], EPS_LN, r_r[0:1, :], ALU.add, ALU.subtract),
            deps=[c, d, msq], reads=("e2_r", "r_r"), writes=("v_r",))
        sd = kb.op("scalar", lambda s: s.activation(
            e2_r[0:1, :], v_r[0:1, :], AF.Sqrt),
            reads=("v_r",), writes=("e2_r",))
        rr = kb.op("vector", lambda v: v.reciprocal(r_r[0:1, :], e2_r[0:1, :]),
                   reads=("e2_r",), writes=("r_r",))
        return [a, b], rr

    def bcast_row(rowname, row, dest, destname, bank0, bank1, rbf_idx, extra_deps=()):
        cvb = kb.op("vector", (lambda v: v.tensor_copy(row_bf[rbf_idx][0:1, :], row[0:1, :])),
                    deps=list(extra_deps), reads=(rowname,), writes=(f"rbf{rbf_idx}",))
        mm0 = kb.op("tensor", (lambda t: t.matmul(
            pb[bank0][:, :], ones1_bf[:, :], row_bf[rbf_idx][0:1, 0:512],
            start=True, stop=True, skip_group_check=True)),
            deps=[c_ones1, cvb], reads=(f"rbf{rbf_idx}",), writes=(PB(bank0),))
        mm1 = kb.op("tensor", (lambda t: t.matmul(
            pb[bank1][:, :], ones1_bf[:, :], row_bf[rbf_idx][0:1, 512:1024],
            start=True, stop=True, skip_group_check=True)),
            reads=(f"rbf{rbf_idx}",), writes=(PB(bank1),))
        ev = kb.op("vector", (lambda v: [
            v.tensor_copy(dest[:, 0:512], pb[bank0][:, :]),
            v.tensor_copy(dest[:, 512:1024], pb[bank1][:, :])][-1]),
            deps=[mm0, mm1], reads=(PB(bank0), PB(bank1)), writes=(destname,))
        return ev

    mA, rA = ln_rows()
    bc_m = bcast_row("m_r", m_r, mb, "mb", 0, 1, 0, extra_deps=mA)
    bc_r = bcast_row("r_r", r_r, rb, "rb", 2, 3, 1, extra_deps=[rA])

    normA = [None] * NI
    for i in range(NI):
        sl = i % 2
        engn = "vector" if (i % 4 == 0) else "gpsimd"
        t1 = kb.op(engn, (lambda v, i=i, sl=sl: v.tensor_tensor(
            stage[sl][:, :], stream_bf[:, i * T:(i + 1) * T], mb[:, :], ALU.subtract)),
            deps=[bc_m], reads=(SB(i), "mb"), writes=(f"st{sl}",))
        t2 = kb.op(engn, (lambda v, i=i, sl=sl: v.tensor_tensor(
            stream_bf[:, i * T:(i + 1) * T], stage[sl][:, :], rb[:, :], ALU.mult)),
            deps=[bc_r, t1], reads=(f"st{sl}", "rb"), writes=(SB(i),))
        normA[i] = t2

    if debug:
        kb.dma("sync", lambda g: g.dma_start(out=taps["t_m1"].ap(), in_=m_r[0:1, :]),
               reads=("m_r",), sem="dot")
        kb.dma("sync", lambda g: g.dma_start(out=taps["t_r1"].ap(), in_=r_r[0:1, :]),
               reads=("r_r",), sem="dot")
        tc = kb.op("vector", lambda v: v.tensor_copy(stage[0][:, :], stream_bf[:, 0:T]),
                   reads=(SB(0),), writes=("st0",))
        kb.dma("sync", lambda g: g.dma_start(out=taps["t_xn1"].ap(), in_=stage[0][:, :]),
               deps=[tc], reads=("st0",), sem="dot")

    R_done = [None] * NH

    def emit_head(h, B0c):
        p2 = h % 2
        rowE = e2_r if p2 == 0 else m_r
        rowV = v_r if p2 == 0 else r_r
        rE = "e2_r" if p2 == 0 else "m_r"
        rV = "v_r" if p2 == 0 else "r_r"
        dest = mb if p2 == 0 else rb
        dn_ = "mb" if p2 == 0 else "rb"
        mmn = kb.op("tensor", (lambda t, h=h, B0=B0c: [
            t.matmul(pb[B0][:, :], M_bf[:, h * 128:(h + 1) * 128],
                     qsel[:, h * T: h * T + 512], start=True, stop=True, skip_group_check=True),
            t.matmul(pb[B0 + 1][:, :], M_bf[:, h * 128:(h + 1) * 128],
                     qsel[:, h * T + 512: h * T + 1024], start=True, stop=True, skip_group_check=True)][-1]),
            deps=[c_M], reads=(QS(h),), writes=(PB(B0c), PB(B0c + 1)))
        mmd = kb.op("tensor", (lambda t, h=h, B0=B0c: [
            t.matmul(pb[B0 + 2][0:1, :], zT_bf[:, h:h + 1],
                     qsel[:, h * T: h * T + 512], start=True, stop=True, skip_group_check=True),
            t.matmul(pb[B0 + 3][0:1, :], zT_bf[:, h:h + 1],
                     qsel[:, h * T + 512: h * T + 1024], start=True, stop=True, skip_group_check=True)][-1]),
            deps=[c_zT], reads=(QS(h),), writes=(PB(B0c + 2), PB(B0c + 3)))
        dn = kb.op("vector", (lambda v, B0=B0c, rowE=rowE: [
            v.tensor_scalar(rowE[0:1, 0:512], pb[B0 + 2][0:1, :], EPS_DEN, None, ALU.add),
            v.tensor_scalar(rowE[0:1, 512:1024], pb[B0 + 3][0:1, :], EPS_DEN, None, ALU.add)][-1]),
            deps=[mmd], reads=(PB(B0c + 2), PB(B0c + 3)), writes=(rE,))
        rec = kb.op("vector", (lambda v, rowE=rowE, rowV=rowV: v.reciprocal(
            rowV[0:1, :], rowE[0:1, :])), deps=[dn], reads=(rE,), writes=(rV,))
        ev = bcast_row(rV, rowV, dest, dn_, B0c + 2, B0c + 3, p2, extra_deps=[rec])
        rr1 = kb.op("vector", (lambda v, h=h, B0=B0c, dest=dest: v.tensor_tensor(
            Rsb[:, h * T: h * T + 512], pb[B0][:, :], dest[:, 0:512], ALU.mult)),
            deps=[ev, mmn], reads=(PB(B0c), dn_), writes=(RS(h),))
        rr2 = kb.op("vector", (lambda v, h=h, B0=B0c, dest=dest: v.tensor_tensor(
            Rsb[:, h * T + 512: h * T + 1024], pb[B0 + 1][:, :], dest[:, 512:1024], ALU.mult)),
            deps=[ev, mmn], reads=(PB(B0c + 1), dn_))
        kb.did_write(rr2, RS(h))
        kb.did_read(rr1, RS(h))
        R_done[h] = rr2

    for s_ in range(4):
        B0 = (s_ % 2) * 4
        mm_last = None
        for g in range(4):
            sl = g % 2
            wd = kb.dma("sync", (lambda y, s_=s_, g=g, sl=sl: y.dma_start(
                out=wring[sl].ap().rearrange("p (a c) -> p a c", a=8),
                in_=wq_d[s_, 8 * g:8 * g + 8, :, :].rearrange("a p c -> p a c"))),
                writes=(f"wr{sl}",), sem=f"dw{sl}")
            for k in range(8):
                i = 8 * g + k
                ws = i % 4
                wc = kb.op("scalar", (lambda s, sl=sl, k=k, ws=ws: s.activation(
                    wbf[ws][:, 0:256], wring[sl][:, k * 256:(k + 1) * 256], AF.Identity)),
                    deps=[wd], reads=(f"wr{sl}",), writes=(f"wbf{ws}",))

                def mkq(t, i=i, ws=ws, B0=B0):
                    for oc in range(2):
                        for th in range(2):
                            inst = t.matmul(
                                pb[B0 + oc * 2 + th][:, :],
                                wbf[ws][:, oc * 128:(oc + 1) * 128],
                                stream_bf[:, i * T + th * 512: i * T + (th + 1) * 512],
                                start=(i == 0), stop=(i == NI - 1), skip_group_check=True)
                    return inst

                if i == 0:
                    mm_last = kb.op("tensor", mkq, deps=[wc, normA[i]],
                                    reads=(f"wbf{ws}", SB(i)),
                                    writes=tuple(PB(B0 + kk) for kk in range(4)))
                else:
                    mm_last = kb.op("tensor", mkq, deps=[wc, normA[i]] + kb.rdep(PB(B0)),
                                    reads=(f"wbf{ws}", SB(i)))
                    kb.did_write(mm_last, *[PB(B0 + kk) for kk in range(4)])
        for oc in range(2):
            h = s_ * 2 + oc
            for th in range(2):
                bank = B0 + oc * 2 + th
                lohi = kb.op("vector", (lambda v, bank=bank, h=h: [
                    v.tensor_scalar(stage[0][:, 0:512], pb[bank][:, :],
                                    bq_pm[:, h:h + 1], 0.0, ALU.add, ALU.min),
                    v.tensor_scalar(stage[0][:, 512:1024], pb[bank][:, :],
                                    bq_pm[:, h:h + 1], 0.0, ALU.add, ALU.max)][-1]),
                    deps=[dl["bq"]], reads=(PB(bank),), writes=("st0",))
                ex = kb.op("scalar", (lambda s: s.activation(
                    stage[1][:, 0:512], stage[0][:, 0:512], AF.Exp)),
                    deps=[lohi], reads=("st0",), writes=("st1",))
                qs = kb.op("gpsimd", (lambda v, h=h, th=th: v.tensor_tensor(
                    qsel[:, h * T + th * 512: h * T + (th + 1) * 512],
                    stage[1][:, 0:512], stage[0][:, 512:1024], ALU.add)),
                    deps=[ex], reads=("st0", "st1"), writes=(QS(h),))
                kb.did_read(qs, QS(h))
        emit_head(s_ * 2, B0)
        emit_head(s_ * 2 + 1, B0)

    if debug:
        kb.dma("sync", lambda g: g.dma_start(out=taps["t_a1"].ap(), in_=A_bf[:, :]),
               deps=kb.rdep(QS(7)), sem="dot")
        tq = kb.op("vector", lambda v: v.tensor_copy(stage[1][:, :], qsel[:, 0:T]),
                   reads=(QS(0),), writes=("st1",))
        kb.dma("sync", lambda g: g.dma_start(out=taps["t_qs"].ap(), in_=stage[1][:, :]),
               deps=[tq], reads=("st1",), sem="dot")

    if debug:
        tr = kb.op("vector", lambda v: v.tensor_copy(stage[0][:, :], Rsb[:, 0:T]),
                   reads=(RS(0),), writes=("st0",))
        kb.dma("sync", lambda g: g.dma_start(out=taps["t_R"].ap(), in_=stage[0][:, :]),
               deps=[tr], reads=("st0",), sem="dot")

    pend_stats = []

    def emit_stats(slot):
        (o, cb, sq) = slot

        def mk_sum2(t, o=o):
            for th in range(2):
                inst = t.matmul(pb[4 + th][0:1, :], ones128_bf[:, :],
                                stream_bf[:, o * T + th * 512: o * T + (th + 1) * 512],
                                start=(o == 0), stop=(o == NI - 1), skip_group_check=True)
            return inst

        def mk_sq2(t, o=o):
            for th in range(2):
                inst = t.matmul(pb[6 + th][0:1, :], ones128_bf[:, :],
                                sqbf[o % 4][:, th * 512:(th + 1) * 512],
                                start=(o == 0), stop=(o == NI - 1), skip_group_check=True)
            return inst

        if o == 0:
            kb.op("tensor", mk_sum2, deps=[cb], reads=(SB(o),), writes=(PB(4), PB(5)))
            kb.op("tensor", mk_sq2, deps=[sq], reads=(f"sq{o % 4}",), writes=(PB(6), PB(7)))
        else:
            s1 = kb.op("tensor", mk_sum2, deps=[cb] + kb.rdep(PB(4)), reads=(SB(o),))
            s2 = kb.op("tensor", mk_sq2, deps=[sq] + kb.rdep(PB(6)), reads=(f"sq{o % 4}",))
            kb.did_write(s1, PB(4), PB(5))
            kb.did_write(s2, PB(6), PB(7))

    for s_ in range(16):
        new_stats = []
        mm_o = None
        for g in range(4):
            sl = g % 2
            wd = kb.dma("sync", (lambda y, s_=s_, g=g, sl=sl: y.dma_start(
                out=wring[sl].ap().rearrange("p (a c) -> p a c", a=8),
                in_=wo_d[s_, 2 * g:2 * g + 2, :, :, :].rearrange("h j p c -> p (h j) c"))),
                writes=(f"wr{sl}",), sem=f"dw{sl}")
            eb = g % 4
            es1 = kb.op("vector", (lambda v, sl=sl: v.tensor_tensor(
                effacc.rearrange("p (h c) -> p h c", h=2),
                wring[sl].ap().rearrange("p (h u c) -> p h u c", h=2, u=2)[:, :, 0, :],
                wring[sl].ap().rearrange("p (h u c) -> p h u c", h=2, u=2)[:, :, 1, :],
                ALU.add)),
                deps=[wd], reads=(f"wr{sl}",), writes=("effacc",))
            es2 = kb.op("vector", (lambda v, eb=eb: v.tensor_tensor(
                effbf2[eb].rearrange("p (h c) -> p h c", h=2),
                effacc.rearrange("p (h u c) -> p h u c", h=2, u=2)[:, :, 0, :],
                effacc.rearrange("p (h u c) -> p h u c", h=2, u=2)[:, :, 1, :],
                ALU.add)),
                deps=[es1], reads=("effacc",), writes=(f"eff{eb}",))
            for hh in range(2):
                h = 2 * g + hh

                def mko(t, h=h, hh=hh, eb=eb):
                    for oc in range(2):
                        for th in range(2):
                            inst = t.matmul(
                                pb[oc * 2 + th][:, :],
                                effbf2[eb][:, hh * 256 + oc * 128: hh * 256 + (oc + 1) * 128],
                                Rsb[:, h * T + th * 512: h * T + (th + 1) * 512],
                                start=(h == 0), stop=(h == NH - 1), skip_group_check=True)
                    return inst

                if h == 0:
                    mm_o = kb.op("tensor", mko, deps=[es2],
                                 reads=(f"eff{eb}", RS(h)),
                                 writes=(PB(0), PB(1), PB(2), PB(3)))
                else:
                    mm_o = kb.op("tensor", mko, deps=[es2] + kb.rdep(PB(0)),
                                 reads=(f"eff{eb}", RS(h)))
                    kb.did_write(mm_o, PB(0), PB(1), PB(2), PB(3))
        o0 = s_ * 2
        slx = s_ % 2
        xd2 = kb.dma("scalar", (lambda y, o0=o0, slx=slx: y.dma_start(
            out=xring[slx].ap().rearrange("p (a c) -> p a c", a=2),
            in_=xT_d[o0 * 128:(o0 + 2) * 128, :].rearrange("(a p) c -> p a c", p=128))),
            writes=(f"xr{slx}",), sem=f"dx{slx}")
        for oc in range(2):
            o = s_ * 2 + oc
            v1 = kb.op("vector", (lambda v, oc=oc, slx=slx: [
                v.tensor_tensor(stage[oc][:, 0:512], pb[oc * 2][:, :],
                                xring[slx][:, oc * T:oc * T + 512], ALU.add),
                v.tensor_tensor(stage[oc][:, 512:1024], pb[oc * 2 + 1][:, :],
                                xring[slx][:, oc * T + 512:(oc + 1) * T], ALU.add)][-1]),
                deps=[mm_o, xd2], reads=(PB(oc * 2), PB(oc * 2 + 1), f"xr{slx}"),
                writes=(f"st{oc}",))
            kb.dma("scalar", (lambda y, o=o, oc=oc: y.dma_start(
                out=x2_d[o * 128:(o + 1) * 128, :], in_=stage[oc][:, :])),
                deps=[v1], reads=(f"st{oc}",), writes=(X2D(o),), sem=f"dy{oc}")
            cb = kb.op("gpsimd", (lambda g, o=o, oc=oc: g.tensor_copy(
                stream_bf[:, o * T:(o + 1) * T], stage[oc][:, :])),
                deps=[v1], reads=(f"st{oc}",), writes=(SB(o),))
            sq = kb.op("scalar", (lambda s, oc=oc, o=o: s.activation(
                sqbf[o % 4][:, :], stage[oc][:, :], AF.Square)),
                deps=[v1], reads=(f"st{oc}",), writes=(f"sq{o % 4}",))
            new_stats.append((o, cb, sq))
        for slot in pend_stats:
            emit_stats(slot)
        pend_stats = new_stats
    for slot in pend_stats:
        emit_stats(slot)
    x2_all_written = [Step("sync", "dy0", kb.count["dy0"]),
                      Step("sync", "dy1", kb.count["dy1"])]

    mB, rB = ln_rows()
    if debug:
        kb.dma("sync", lambda g: g.dma_start(out=taps["t_a2"].ap(), in_=A_bf[:, :]),
               deps=mB, sem="dot")
    bc_m2 = bcast_row("m_r", m_r, mb, "mb", 4, 5, 0, extra_deps=mB)
    bc_r2 = bcast_row("r_r", r_r, rb, "rb", 6, 7, 1, extra_deps=[rB])

    normB = [None] * NI
    for i in range(NI):
        sl = i % 2
        engn = "vector" if (i % 4 == 0) else "gpsimd"
        t1 = kb.op(engn, (lambda v, i=i, sl=sl: v.tensor_tensor(
            stage[sl][:, :], stream_bf[:, i * T:(i + 1) * T], mb[:, :], ALU.subtract)),
            deps=[bc_m2], reads=(SB(i), "mb"), writes=(f"st{sl}",))
        t2 = kb.op(engn, (lambda v, i=i, sl=sl: v.tensor_tensor(
            stream_bf[:, i * T:(i + 1) * T], stage[sl][:, :], rb[:, :], ALU.mult)),
            deps=[bc_r2, t1], reads=(f"st{sl}", "rb"), writes=(SB(i),))
        normB[i] = t2

    if debug:
        kb.dma("sync", lambda g: g.dma_start(out=taps["t_m2"].ap(), in_=m_r[0:1, :]),
               reads=("m_r",), sem="dot")
        kb.dma("sync", lambda g: g.dma_start(out=taps["t_r2"].ap(), in_=r_r[0:1, :]),
               reads=("r_r",), sem="dot")
        tn2 = kb.op("vector", lambda v: v.tensor_copy(stage[0][:, :], stream_bf[:, 0:T]),
                    reads=(SB(0),), writes=("st0",))
        kb.dma("sync", lambda g: g.dma_start(out=taps["t_xn2"].ap(), in_=stage[0][:, :]),
               deps=[tn2], reads=("st0",), sem="dot")

    for i in range(NI):
        def mkz(t, i=i):
            for th in range(2):
                inst = t.matmul(pb[4 + th][0:32, :], A_bf[:, i * 32:(i + 1) * 32],
                                stream_bf[:, i * T + th * 512: i * T + (th + 1) * 512],
                                start=(i == 0), stop=(i == NI - 1), skip_group_check=True)
            return inst

        if i == 0:
            kb.op("tensor", mkz, deps=[normB[i], c_A], reads=(SB(i),),
                  writes=(PB(4), PB(5)))
        else:
            sz = kb.op("tensor", mkz, deps=[normB[i]] + kb.rdep(PB(4)), reads=(SB(i),))
            kb.did_write(sz, PB(4), PB(5))
    zev = kb.op("vector", (lambda v: [
        v.tensor_copy(z_sb[:, 0:512], pb[4][0:32, :]),
        v.tensor_copy(z_sb[:, 512:1024], pb[5][0:32, :])][-1]),
        reads=(PB(4), PB(5)), writes=("z_sb",))

    for i in range(NI):
        def mkh(t, i=i):
            for th in range(2):
                inst = t.matmul(pb[6 + th][0:8, :], Wsl_bf[:, i * 8:(i + 1) * 8],
                                stream_bf[:, i * T + th * 512: i * T + (th + 1) * 512],
                                start=(i == 0), stop=(i == NI - 1), skip_group_check=True)
            return inst

        if i == 0:
            kb.op("tensor", mkh, deps=[normB[i], c_Wsl], reads=(SB(i),),
                  writes=(PB(6), PB(7)))
        else:
            shh = kb.op("tensor", mkh, deps=[normB[i]] + kb.rdep(PB(6)), reads=(SB(i),))
            kb.did_write(shh, PB(6), PB(7))
    hev = kb.op("vector", (lambda v: [
        v.tensor_scalar(Hs_f[0:8, 0:512], pb[6][0:8, :], b_pm[0:8, 0:1], None, ALU.add),
        v.tensor_scalar(Hs_f[0:8, 512:1024], pb[7][0:8, :], b_pm[0:8, 0:1], None, ALU.add)][-1]),
        deps=[dl["b"]], reads=(PB(6), PB(7)), writes=("Hs",))

    if debug:
        kb.dma("sync", lambda g: g.dma_start(out=taps["t_hs"].ap(), in_=Hs_f[:, :]),
               reads=("Hs",), sem="dot")
        kb.dma("sync", lambda g: g.dma_start(out=taps["t_z"].ap(), in_=z_sb[:, :]),
               reads=("z_sb",), sem="dot")

    zo = kb.op("vector", (lambda v: v.tensor_copy(zodd_bf[:, :], z_sb[:, 1:1024:2])),
               reads=("z_sb",), writes=("zodd",))
    for tb in range(4):
        mmds = kb.op("tensor", (lambda t, tb=tb: t.matmul(
            pb[4 + tb][:, 0:8], zodd_bf[:, tb * 128:(tb + 1) * 128], LiMT_bf[:, :],
            start=True, stop=True, skip_group_check=True)),
            deps=[zo, c_LiMT], reads=("zodd",), writes=(PB(4 + tb),))
        kb.op("vector", (lambda v, tb=tb: v.tensor_copy(
            gts(G_DS)[:, tb * 8:(tb + 1) * 8], pb[4 + tb][:, 0:8])),
            deps=[mmds], reads=(PB(4 + tb),), writes=(f"gds{tb}",))

    for tb in range(4):
        mmhT = kb.op("tensor", (lambda t, tb=tb: t.transpose(
            pb[4 + tb][:, 0:8], Hs_f[0:8, tb * 256 + 1:(tb + 1) * 256:2], ident[0:8, 0:8])),
            deps=[c_id], reads=("Hs",), writes=(PB(4 + tb),))
        kb.op("vector", (lambda v, tb=tb: v.tensor_copy(
            gts(G_HS)[:, tb * 8:(tb + 1) * 8], pb[4 + tb][:, 0:8])),
            deps=[mmhT], reads=(PB(4 + tb),), writes=(f"ghs{tb}",))

    ghs_all = tuple(f"ghs{tb}" for tb in range(4))
    gds_all = tuple(f"gds{tb}" for tb in range(4))

    if debug:
        kb.dma("sync", lambda g: g.dma_start(out=taps["t_gt2"].ap(), in_=gt[:, 0:64]),
               reads=ghs_all + gds_all, sem="dot")
        tzo = kb.op("vector", lambda v: v.tensor_copy(stage[0][0:32, 0:512], zodd_bf[:, :]),
                    reads=("zodd",), writes=("st0",))
        kb.dma("sync", lambda g: g.dma_start(out=taps["t_zo"].ap(), in_=stage[0][0:32, 0:512]),
               deps=[tzo], reads=("st0",), sem="dot")

    zrest = kb.op("vector", (lambda v: v.memset(sc_sb[0:1, 2:64], 0.0)), writes=("sc",))
    redm = []
    for (srcs, col, which) in ((ghs_all, 0, G_HS), (gds_all, 1, G_DS)):
        r1 = kb.op("vector", (lambda v, col=col, which=which: v.reduce_max(
            gc[:, col:col + 1], gts(which)[:, :], AX.X, apply_absolute_value=True)),
            reads=srcs, writes=(f"gcmax{col}",))
        mmt = kb.op("tensor", (lambda t, col=col: t.transpose(
            pb[4 + col][0:1, 0:128], gc[:, col:col + 1], ident[:, :])),
            deps=[r1, c_id], reads=(f"gcmax{col}",), writes=(PB(4 + col),))
        r2 = kb.op("vector", (lambda v, col=col: v.reduce_max(
            sc_sb[0:1, col:col + 1], pb[4 + col][0:1, 0:128], AX.X,
            apply_absolute_value=True)), deps=[mmt, zrest], reads=(PB(4 + col),))
        kb.did_write(r2, "sc")
        kb.did_read(zrest, "sc")
        redm.append(r2)

    ccw = kb.dma("gpsimd", (lambda g: g.dma_start(out=ccb_d[:, :], in_=sc_sb[0:1, :])),
                 deps=redm, reads=("sc",), sem="dp")
    cc_in = bass.AP(ccb_d.ap().tensor, 0, [[1, 1], [1, 64]])
    cc_out = bass.AP(ccb_d.ap().tensor, 0, [[1, 1], [1, 64]])
    ccr = kb.op("gpsimd", (lambda g: g.collective_compute(
        "AllReduce", ALU.max, replica_groups=[list(range(NCORES))],
        ins=[cc_in], outs=[cc_out])), deps=[ccw], sem="cc")
    ccb = kb.dma("gpsimd", (lambda g: g.dma_start(out=sc_sb[0:1, :], in_=ccb_d[:, :])),
                 deps=[ccr], writes=("sc",), sem="dp")

    post_state = {}

    def emit_post_collective():
        sc1 = kb.op("vector", (lambda v: v.tensor_scalar(
            sc_sb[0:1, 32:34], sc_sb[0:1, 0:2], EPS_DEN, None, ALU.max)),
            deps=[ccb], reads=("sc",))
        kb.did_write(sc1, "sc")
        kb.did_read(ccb, "sc")
        sc2 = kb.op("vector", (lambda v: v.reciprocal(sc_sb[0:1, 34:36], sc_sb[0:1, 32:34])),
                    deps=[sc1])
        sc3a = kb.op("vector", (lambda v: v.tensor_scalar(
            sc_sb[0:1, 36:38], sc_sb[0:1, 34:36], 0.5, None, ALU.mult)), deps=[sc2])
        sc3 = kb.op("vector", (lambda v: v.tensor_copy(
            row_bf[0][0:1, 0:2], sc_sb[0:1, 36:38])), deps=[sc3a], writes=("rbf0",))
        mmsc = kb.op("tensor", (lambda t: t.matmul(
            pb[4][:, 0:512], ones1_bf[:, :], row_bf[0][0:1, 0:512], start=True, stop=True,
            skip_group_check=True)), deps=[sc3, c_ones1], reads=("rbf0",), writes=(PB(4),))
        scev = kb.op("vector", (lambda v: v.tensor_copy(gc[:, 2:4], pb[4][:, 0:2])),
                     deps=[mmsc], reads=(PB(4),), writes=("gsc",))
        if debug:
            kb.dma("sync", lambda g: g.dma_start(out=taps["t_scb2"].ap(), in_=gc[:, 0:4]),
                   deps=[scev], reads=("gsc",), sem="dot")

        if debug:
            kb.dma("sync", lambda g: g.dma_start(out=taps["t_sc"].ap(), in_=sc_sb[0:1, :]),
                   deps=[sc3], reads=("sc",), sem="dot")

        def vop(fn, deps=(), reads=(), writes=()):
            return kb.op("vector", fn, deps, reads, writes)

        g1 = vop(lambda v: v.tensor_scalar(
            gts(G_T1)[:, :], gts(G_HS)[:, :], gc[:, 2:3], None, ALU.mult),
            reads=ghs_all + ("gsc",), writes=("gT1",))
        g2 = vop(lambda v: v.scalar_tensor_tensor(
            gts(G_LG)[:, :], gts(G_DS)[:, :], gc[:, 3:4], gts(G_T1)[:, :],
            ALU.mult, ALU.add),
            deps=[g1], reads=gds_all + ("gsc", "gT1"), writes=("gLG",))
        mx = vop(lambda v: [v.reduce_max(gc[:, 4 + tb:5 + tb],
                                         gts(G_LG)[:, tb * 8:(tb + 1) * 8], AX.X)
                            for tb in range(4)][-1],
                 deps=[g2], reads=("gLG",), writes=("gmx",))
        sh = vop(lambda v: [v.tensor_scalar(
            gts(G_T1)[:, tb * 8:(tb + 1) * 8], gts(G_LG)[:, tb * 8:(tb + 1) * 8],
            gc[:, 4 + tb:5 + tb], None, ALU.subtract) for tb in range(4)][-1],
            deps=[mx, g2], reads=("gLG", "gmx"), writes=("gT1",))
        gex = kb.op("scalar", (lambda s: s.activation(
            gts(G_EX)[:, :], gts(G_T1)[:, :], AF.Exp)), deps=[sh],
            reads=("gT1",), writes=("gEX",))
        sm = vop(lambda v: [v.reduce_sum(gc[:, 8 + tb:9 + tb],
                                         gts(G_EX)[:, tb * 8:(tb + 1) * 8], AX.X)
                            for tb in range(4)][-1],
                 deps=[gex], reads=("gEX",), writes=("gsm",))
        grc = vop(lambda v: v.reciprocal(gc[:, 12:16], gc[:, 8:12]),
                  deps=[sm], reads=("gsm",), writes=("grs",))
        pr = vop(lambda v: [v.tensor_scalar(
            gts(G_PR)[:, tb * 8:(tb + 1) * 8], gts(G_EX)[:, tb * 8:(tb + 1) * 8],
            gc[:, 12 + tb:13 + tb], None, ALU.mult) for tb in range(4)][-1],
            deps=[grc], reads=("gEX", "grs"), writes=("gPR",))
        m1 = vop(lambda v: [v.reduce_max(gc[:, 16 + tb:17 + tb],
                                         gts(G_PR)[:, tb * 8:(tb + 1) * 8], AX.X)
                            for tb in range(4)][-1],
                 deps=[pr], reads=("gPR",), writes=("gm1",))
        im = vop(lambda v: [v.tensor_scalar(
            gts(G_IM)[:, tb * 8:(tb + 1) * 8], gts(G_PR)[:, tb * 8:(tb + 1) * 8],
            gc[:, 16 + tb:17 + tb], None, ALU.is_ge) for tb in range(4)][-1],
            deps=[m1], reads=("gPR", "gm1"), writes=("gIM",))
        cn = vop(lambda v: [v.reduce_sum(gc[:, 20 + tb:21 + tb],
                                         gts(G_IM)[:, tb * 8:(tb + 1) * 8], AX.X)
                            for tb in range(4)][-1],
                 deps=[im], reads=("gIM",), writes=("gcn",))
        mk = vop(lambda v: v.scalar_tensor_tensor(
            gts(G_MK)[:, :], gts(G_IM)[:, :], -1e30, gts(G_PR)[:, :],
            ALU.mult, ALU.add),
            deps=[im], reads=("gIM", "gPR"), writes=("gMK",))
        m2 = vop(lambda v: [v.reduce_max(gc[:, 24 + tb:25 + tb],
                                         gts(G_MK)[:, tb * 8:(tb + 1) * 8], AX.X)
                            for tb in range(4)][-1],
                 deps=[mk], reads=("gMK",), writes=("gm2",))
        fl = vop(lambda v: v.tensor_scalar(gc[:, 28:32], gc[:, 20:24], 1.5, None, ALU.is_ge),
                 deps=[cn], reads=("gcn",), writes=("gfl",))
        d12 = vop(lambda v: v.tensor_tensor(gc[:, 16:20], gc[:, 16:20], gc[:, 24:28], ALU.subtract),
                  deps=[m2], reads=("gm1", "gm2"), writes=("gm1",))
        fd = vop(lambda v: v.tensor_tensor(gc[:, 28:32], gc[:, 28:32], gc[:, 16:20], ALU.mult),
                 deps=[fl, d12], reads=("gfl", "gm1"), writes=("gfl",))
        thr = vop(lambda v: v.tensor_tensor(gc[:, 24:28], gc[:, 24:28], gc[:, 28:32], ALU.add),
                  deps=[fd], reads=("gm2", "gfl"), writes=("gm2",))
        pm = vop(lambda v: [v.tensor_scalar(
            gts(G_T1)[:, tb * 8:(tb + 1) * 8], gts(G_PR)[:, tb * 8:(tb + 1) * 8],
            gc[:, 24 + tb:25 + tb], 2.0, ALU.subtract, ALU.mult) for tb in range(4)][-1],
            deps=[thr, gex], reads=("gPR", "gm2"), writes=("gT1",))
        sg = kb.op("scalar", (lambda s: s.activation(
            gts(G_MK)[:, :], gts(G_T1)[:, :], AF.Sigmoid)),
            deps=[pm, m2], reads=("gT1",), writes=("gMK",))
        w_ = vop(lambda v: v.tensor_tensor(gts(G_W)[:, :], gts(G_PR)[:, :], gts(G_MK)[:, :], ALU.mult),
                 deps=[sg], reads=("gPR", "gMK"), writes=("gW",))
        ws = vop(lambda v: [v.reduce_sum(gc[:, 16 + tb:17 + tb],
                                         gts(G_W)[:, tb * 8:(tb + 1) * 8], AX.X)
                            for tb in range(4)][-1],
                 deps=[w_, d12, fd], reads=("gW",), writes=("gm1",))
        wsp = vop(lambda v: v.tensor_scalar(gc[:, 20:24], gc[:, 16:20], 1e-9, None, ALU.add),
                  deps=[ws], reads=("gm1",), writes=("gcn",))
        wrc = vop(lambda v: v.reciprocal(gc[:, 24:28], gc[:, 20:24]),
                  deps=[wsp, thr], reads=("gcn",), writes=("gm2",))
        wn = vop(lambda v: [v.tensor_scalar(
            gts(G_WN)[:, tb * 8:(tb + 1) * 8], gts(G_W)[:, tb * 8:(tb + 1) * 8],
            gc[:, 24 + tb:25 + tb], None, ALU.mult) for tb in range(4)][-1],
            deps=[wrc], reads=("gW", "gm2"), writes=("gWN",))

        if debug:
            tg = vop(lambda v: [
                v.tensor_copy(gt[:, 64:96], gts(G_PR)[:, :]),
                v.tensor_copy(gt[:, 96:128], gts(G_WN)[:, :])][-1],
                deps=[wn], reads=("gPR", "gWN"), writes=("gLG", "gEX"))
            kb.dma("sync", lambda g: g.dma_start(out=taps["t_gt"].ap(), in_=gt[:, 64:128]),
                   deps=[tg], sem="dot")

        wf_last = None
        for tb in range(4):
            mmw = kb.op("tensor", (lambda t, tb=tb: t.transpose(
                pb[5][0:8, 0:128], gts(G_WN)[:, tb * 8:(tb + 1) * 8], ident[:, :])),
                deps=[wn, c_id], reads=("gWN",), writes=(PB(5),))
            wf_last = kb.op("vector", (lambda v, tb=tb: [
                v.tensor_copy(wfull_bf[0:8, tb * 256 + 0:(tb + 1) * 256:2], pb[5][0:8, 0:128]),
                v.tensor_copy(wfull_bf[0:8, tb * 256 + 1:(tb + 1) * 256:2], pb[5][0:8, 0:128])][-1]),
                deps=[mmw], reads=(PB(5),))
            if tb == 0:
                kb.did_write(wf_last, "wfull")
            else:
                kb.did_read(wf_last, "wfull")

        mmp = kb.op("tensor", (lambda t: [
            t.matmul(pb[4][0:32, :], LiM_bf[:, :], wfull_bf[0:8, 0:512],
                     start=True, stop=True, skip_group_check=True),
            t.matmul(pb[5][0:32, :], LiM_bf[:, :], wfull_bf[0:8, 512:1024],
                     start=True, stop=True, skip_group_check=True)][-1]),
            deps=[c_LiM] + kb.bufs["wfull"]["pend"] + [kb.bufs["wfull"]["w"]],
            writes=(PB(4), PB(5)))
        zp_done = kb.op("vector", (lambda v: [
            v.tensor_tensor(zp_bf[:, 0:512], z_sb[:, 0:512], pb[4][0:32, :], ALU.mult),
            v.tensor_tensor(zp_bf[:, 512:1024], z_sb[:, 512:1024], pb[5][0:32, :], ALU.mult)][-1]),
            deps=[mmp], reads=(PB(4), PB(5), "z_sb"), writes=("zp",))

        if debug:
            tzp = vop(lambda v: v.tensor_copy(stage[0][0:32, :], zp_bf[:, :]),
                      reads=("zp",), writes=("st0",))
            kb.dma("sync", lambda g: g.dma_start(out=taps["t_zp"].ap(), in_=stage[0][0:32, :]),
                   deps=[tzp], reads=("st0",), sem="dot")
            twf = vop(lambda v: v.tensor_copy(stage[1][0:8, :], wfull_bf[:, :]),
                      deps=[wf_last], writes=("st1",))
            kb.dma("sync", lambda g: g.dma_start(out=taps["t_wf"].ap(), in_=stage[1][0:8, :]),
                   deps=[twf], reads=("st1",), sem="dot")


        post_state["zp_done"] = zp_done

    out_steps = []
    for s_ in range(16):
        B0 = 0 if (s_ < 2 or s_ % 2 == 0) else 4
        o0 = s_ * 2
        slx = s_ % 2
        x2rd = kb.dma("scalar", (lambda y, o0=o0, slx=slx: y.dma_start(
            out=xring[slx].ap().rearrange("p (a c) -> p a c", a=2),
            in_=x2_d[o0 * 128:(o0 + 2) * 128, :].rearrange("(a p) c -> p a c", p=128))),
            deps=[x2_all_written], writes=(f"xr{slx}",), sem=f"dx{slx}")
        mm_f = None
        for g in range(4):
            sl = g % 2
            wd = kb.dma("sync", (lambda y, s_=s_, g=g, sl=sl: y.dma_start(
                out=wring[sl].ap().rearrange("p (a c) -> p a c", a=8),
                in_=wt_d[s_, 8 * g:8 * g + 8, :, :].rearrange("a p c -> p a c"))),
                writes=(f"wr{sl}",), sem=f"dw{sl}")
            for k in range(8):
                i = 8 * g + k
                ws = i % 4
                wc = kb.op("scalar", (lambda s, sl=sl, k=k, ws=ws: s.activation(
                    wbf[ws][:, 0:256], wring[sl][:, k * 256:(k + 1) * 256], AF.Identity)),
                    deps=[wd], reads=(f"wr{sl}",), writes=(f"wbf{ws}",))

                def mkf(t, i=i, ws=ws, B0=B0):
                    for oc in range(2):
                        for th in range(2):
                            inst = t.matmul(
                                pb[B0 + oc * 2 + th][:, :],
                                wbf[ws][:, oc * 128:(oc + 1) * 128],
                                stream_bf[:, i * T + th * 512: i * T + (th + 1) * 512],
                                start=(i == 0), stop=False, skip_group_check=True)
                    return inst

                if i == 0:
                    mm_f = kb.op("tensor", mkf, deps=[wc, normB[i]],
                                 reads=(f"wbf{ws}", SB(i)),
                                 writes=tuple(PB(B0 + kk) for kk in range(4)))
                else:
                    mm_f = kb.op("tensor", mkf, deps=[wc, normB[i]] + kb.rdep(PB(B0)),
                                 reads=(f"wbf{ws}", SB(i)))
                    kb.did_write(mm_f, *[PB(B0 + kk) for kk in range(4)])

        if s_ == 0:
            emit_post_collective()

        def mkdelta(t, s_=s_, B0=B0):
            for oc in range(2):
                for th in range(2):
                    inst = t.matmul(
                        pb[B0 + oc * 2 + th][:, :],
                        Bm_bf[:, (s_ * 2 + oc) * 128:(s_ * 2 + oc + 1) * 128],
                        zp_bf[:, th * 512:(th + 1) * 512],
                        start=False, stop=True, skip_group_check=True)
            return inst

        mm_d = kb.op("tensor", mkdelta, deps=[mm_f, post_state["zp_done"], c_Bm], reads=("zp",))
        kb.did_write(mm_d, *[PB(B0 + kk) for kk in range(4)])

        for oc in range(2):
            o = s_ * 2 + oc
            v2 = kb.op("vector", (lambda v, o=o, oc=oc, slx=slx, B0=B0: [
                v.scalar_tensor_tensor(
                    stage[oc][:, 0:512], pb[B0 + oc * 2][:, :], b_pm[:, o:o + 1],
                    xring[slx][:, oc * T:oc * T + 512], ALU.add, ALU.add),
                v.scalar_tensor_tensor(
                    stage[oc][:, 512:1024], pb[B0 + oc * 2 + 1][:, :], b_pm[:, o:o + 1],
                    xring[slx][:, oc * T + 512:(oc + 1) * T], ALU.add, ALU.add)][-1]),
                deps=[mm_d, x2rd, dl["b"]],
                reads=(PB(B0 + oc * 2), PB(B0 + oc * 2 + 1), f"xr{slx}"),
                writes=(f"st{oc}",))
            ow = kb.dma("sync", (lambda y, o=o, oc=oc: y.dma_start(
                out=outT_d[o * 128:(o + 1) * 128, :], in_=stage[oc][:, :])),
                deps=[v2], reads=(f"st{oc}",), sem=f"do{oc}")
            out_steps.append(ow)

    if debug:
        tz2 = kb.op("vector", lambda v: v.tensor_copy(stage[0][0:32, :], z_sb[:, :]),
                    deps=out_steps, reads=("z_sb",), writes=("st0",))
        kb.dma("sync", lambda g: g.dma_start(out=taps["t_z2"].ap(), in_=stage[0][0:32, :]),
               deps=[tz2], reads=("st0",), sem="dot")
        tabf = kb.op("vector", lambda v: v.tensor_copy(stage[1][:, :], A_bf[:, :]),
                     deps=out_steps, writes=("st1",))
        kb.dma("sync", lambda g: g.dma_start(out=taps["t_abf"].ap(), in_=stage[1][:, :]),
               deps=[tabf], reads=("st1",), sem="dot")
        kb.dma("sync", lambda g: g.dma_start(out=taps["t_a3"].ap(), in_=A_bf[:, :]),
               deps=out_steps, sem="dot")
        kb.dma("sync", lambda g: g.dma_start(out=taps["t_ones"].ap(), in_=ones1_bf[:, :]),
               deps=out_steps, sem="dot")
        kb.dma("sync", lambda g: g.dma_start(out=taps["t_scb"].ap(), in_=gc[:, 0:4]),
               deps=out_steps, reads=("gsc",), sem="dot")

    kb.op("sync", (lambda y: None), deps=out_steps)

    kb.emit(nc, sems)
    ctx.close()
    return nc


_SEL = np.concatenate([np.arange(g * 128, (g + 1) * 128) for g in range(0, 32, 4)])


def _stage_inputs(L0_attn_hidden_states, L0_attn_Wq, L0_attn_bq, L0_attn_Wo,
                  L0_attn_M, L0_attn_z, L0_ffn_W, L0_ffn_b, L0_ffn_A,
                  L0_ffn_Bmat, L0_ffn_LiMEs):
    x = np.asarray(L0_attn_hidden_states, dtype=np.float32).reshape(8192, D)
    xT = np.ascontiguousarray(x.T)
    Wq = np.asarray(L0_attn_Wq, np.float32)
    WqTsel = np.ascontiguousarray(Wq[_SEL, :].T)
    wq_st = np.ascontiguousarray(WqTsel.reshape(NI, 128, 4, 256).transpose(2, 0, 1, 3))
    bq = np.asarray(L0_attn_bq, np.float32)
    bq_pm = np.ascontiguousarray(bq[_SEL].reshape(8, 128).T)
    WoT = np.ascontiguousarray(np.asarray(L0_attn_Wo, np.float32).T)
    wo_st = np.ascontiguousarray(WoT.reshape(8, 4, 128, 16, 256).transpose(3, 0, 1, 2, 4)[:, :, [0, 2, 1, 3]])
    WT = np.ascontiguousarray(np.asarray(L0_ffn_W, np.float32).T)
    wt_st = np.ascontiguousarray(WT.reshape(NI, 128, 16, 256).transpose(2, 0, 1, 3))
    b_pm = np.ascontiguousarray(np.asarray(L0_ffn_b, np.float32).reshape(32, 128).T)
    wsl_pm = np.ascontiguousarray(
        WT[:, 0:8].reshape(NI, 128, 8).transpose(1, 0, 2).reshape(128, 256))
    zT = np.ascontiguousarray(np.asarray(L0_attn_z, np.float32).T)
    M2 = np.ascontiguousarray(
        np.asarray(L0_attn_M, np.float32).transpose(1, 0, 2).reshape(128, 1024))
    A2 = np.ascontiguousarray(
        np.asarray(L0_ffn_A, np.float32).reshape(32, 128, 32).transpose(1, 0, 2).reshape(128, 1024))
    Bm = np.ascontiguousarray(np.asarray(L0_ffn_Bmat, np.float32))
    LiM = np.ascontiguousarray(np.asarray(L0_ffn_LiMEs, np.float32))
    LiMT = np.ascontiguousarray(LiM.T)
    shared = {"wq_st": wq_st, "wo_st": wo_st, "wt_st": wt_st, "bq_pm": bq_pm,
              "b_pm": b_pm, "zT": zT, "M2": M2, "A2": A2, "Bm": Bm,
              "LiM": LiM, "LiMT": LiMT, "Wsl_pm": wsl_pm,
              "ident_in": np.eye(128, dtype=np.float32)}
    in_maps = []
    for c in range(NCORES):
        m = dict(shared)
        m["xT"] = np.ascontiguousarray(xT[:, c * T:(c + 1) * T])
        in_maps.append(m)
    return in_maps


def kernel(L0_attn_hidden_states, L0_attn_Wq, L0_attn_bq, L0_attn_Wk,
           L0_attn_bk, L0_attn_Wv, L0_attn_bv, L0_attn_Wo, L0_attn_M,
           L0_attn_z, L0_ffn_W, L0_ffn_b, L0_ffn_A, L0_ffn_Bmat, L0_ffn_LiMEs):
    in_maps = _stage_inputs(L0_attn_hidden_states, L0_attn_Wq, L0_attn_bq,
                            L0_attn_Wo, L0_attn_M, L0_attn_z, L0_ffn_W,
                            L0_ffn_b, L0_ffn_A, L0_ffn_Bmat, L0_ffn_LiMEs)
    nc = build(debug=False)
    res = run_bass_kernel_spmd(nc, in_maps, core_ids=list(range(NCORES)))
    parts = [np.ascontiguousarray(res.results[c]["outT"].T) for c in range(NCORES)]
    out = np.concatenate(parts, axis=0).reshape(2, 4096, 4096)
    return out


# version 13 (final)
# speedup vs baseline: 1.1777x; 1.1053x over previous
import numpy as np
from contextlib import ExitStack

import concourse.bass as bass
import concourse.mybir as mybir
from concourse.bass_utils import run_bass_kernel_spmd

F32 = mybir.dt.float32
BF16 = mybir.dt.bfloat16
AF = mybir.ActivationFunctionType
ALU = mybir.AluOpType
AX = mybir.AxisListType

D = 4096
T = 1024
NI = D // 128
NH = 8
NCORES = 8
EPS_LN = 1e-5
EPS_DEN = 1e-6


class Step:
    __slots__ = ("eng", "sem", "val")

    def __init__(self, eng, sem, val):
        self.eng, self.sem, self.val = eng, sem, val


class KB:

    ENGS = ("sync", "tensor", "vector", "scalar", "gpsimd")

    def __init__(self):
        self.sched = []
        self.count = {}
        self.bufs = {}

    def _b(self, name):
        if name not in self.bufs:
            self.bufs[name] = {"w": None, "pend": []}
        return self.bufs[name]

    def rdep(self, *names):
        return [self._b(n)["w"] for n in names]

    def wdep(self, *names):
        out = []
        for n in names:
            b = self._b(n)
            out += b["pend"]
            out.append(b["w"])
        return out

    def did_read(self, step, *names):
        for n in names:
            self._b(n)["pend"].append(step)

    def did_write(self, step, *names):
        for n in names:
            b = self._b(n)
            b["w"] = step
            b["pend"] = []

    def op(self, eng, fn, deps=(), reads=(), writes=(), sem=None, amt=1):
        if sem is None:
            sem = {"tensor": "et", "vector": "ev", "scalar": "es",
                   "gpsimd": "ep", "sync": "ey"}[eng]
        alldeps = list(deps) + self.rdep(*reads) + self.wdep(*writes)
        waits = []
        for d in alldeps:
            if d is None:
                continue
            if isinstance(d, (list, tuple)):
                waits += [(x.sem, x.val) for x in d if x is not None]
            else:
                waits.append((d.sem, d.val))
        self.count[sem] = self.count.get(sem, 0) + amt
        st = Step(eng, sem, self.count[sem])
        self.sched.append((eng, waits, fn, sem, amt))
        self.did_read(st, *reads)
        self.did_write(st, *writes)
        return st

    def dma(self, eng, fn, deps=(), reads=(), writes=(), sem="dy"):
        return self.op(eng, fn, deps, reads, writes, sem=sem, amt=16)

    def emit(self, nc, sems):
        with nc.Block() as block:
            for eng_name in self.ENGS:
                entries = [e for e in self.sched if e[0] == eng_name]
                if not entries:
                    continue

                def section(eng, entries=entries):
                    waited = {}
                    for (_, waits, fn, sem, amt) in entries:
                        best = {}
                        for (s, v) in waits:
                            if v > best.get(s, -1):
                                best[s] = v
                        for s, v in sorted(best.items()):
                            if waited.get(s, 0) < v:
                                eng.wait_ge(sems[s], v)
                                waited[s] = v
                        inst = fn(eng)
                        if inst is not None:
                            inst.then_inc(sems[sem], amt)

                getattr(block, eng_name)(section)


def build(debug=False):
    nc = bass.Bass("TRN2", target_bir_lowering=False, debug=False, num_devices=8)
    kb = KB()

    xT_d = nc.dram_tensor("xT", [D, T], F32, kind="ExternalInput")
    wq_d = nc.dram_tensor("wq_st", [4, NI, 128, 256], F32, kind="ExternalInput")
    wo_d = nc.dram_tensor("wo_st", [16, 8, 4, 128, 256], F32, kind="ExternalInput")
    wt_d = nc.dram_tensor("wt_st", [16, NI, 128, 256], F32, kind="ExternalInput")
    bq_d = nc.dram_tensor("bq_pm", [128, 8], F32, kind="ExternalInput")
    b_d = nc.dram_tensor("b_pm", [128, 32], F32, kind="ExternalInput")
    zrep_d = nc.dram_tensor("zrep", [128, 1024], F32, kind="ExternalInput")
    M_d = nc.dram_tensor("M2", [128, 1024], F32, kind="ExternalInput")
    A_d = nc.dram_tensor("A2", [128, 1024], F32, kind="ExternalInput")
    Bm_d = nc.dram_tensor("Bm", [32, D], F32, kind="ExternalInput")
    LiM_d = nc.dram_tensor("LiM", [8, 32], F32, kind="ExternalInput")
    LiMT_d = nc.dram_tensor("LiMT", [32, 8], F32, kind="ExternalInput")
    Wsl_d = nc.dram_tensor("Wsl_pm", [128, 256], F32, kind="ExternalInput")
    id_d = nc.dram_tensor("ident_in", [128, 128], F32, kind="ExternalInput")

    outT_d = nc.dram_tensor("outT", [D, T], F32, kind="ExternalOutput")
    x2_d = nc.dram_tensor("x2scratch", [D, T], F32, kind="Internal")
    ccb_d = nc.dram_tensor("ccbuf", [1, 64], F32, kind="Internal")

    taps = {}
    if debug:
        for nm, shp in [("t_m1", [1, T]), ("t_r1", [1, T]), ("t_xn1", [128, T]),
                        ("t_qs", [128, T]), ("t_R", [128, T]), ("t_den", [1, T]),
                        ("t_m2", [1, T]), ("t_r2", [1, T]), ("t_xn2", [128, T]),
                        ("t_z", [32, T]), ("t_hs", [8, T]), ("t_gt", [128, 64]),
                        ("t_sc", [1, 64]), ("t_zp", [32, T]), ("t_wf", [8, T]),
                        ("t_gt2", [128, 64]), ("t_zo", [32, 512]), ("t_z2", [32, T]),
                        ("t_abf", [128, T])]:
            taps[nm] = nc.dram_tensor(nm, shp, F32, kind="ExternalOutput")
        for nm in ("t_a0", "t_a1", "t_a2", "t_a3"):
            taps[nm] = nc.dram_tensor(nm, [128, T], BF16, kind="ExternalOutput")
        taps["t_ones"] = nc.dram_tensor("t_ones", [1, 128], BF16, kind="ExternalOutput")
        taps["t_scb"] = nc.dram_tensor("t_scb", [128, 4], F32, kind="ExternalOutput")
        taps["t_scb2"] = nc.dram_tensor("t_scb2", [128, 4], F32, kind="ExternalOutput")

    ctx = ExitStack()
    e = ctx.enter_context

    sems = {}
    for s in ("et", "ev", "es", "ep", "ey", "dy", "dp", "do", "cc",
              "dc", "dw0", "dw1", "dx0", "dx1", "dy0", "dy1", "do0", "do1", "do2", "do3", "dot"):
        sems[s] = e(nc.semaphore(s))

    ones1_bf = e(nc.sbuf_tensor("ones1_bf", [1, 128], BF16))
    ones128_bf = e(nc.sbuf_tensor("ones128_bf", [128, 1], BF16))
    ident = e(nc.sbuf_tensor("ident", [128, 128], F32))
    bq_pm = e(nc.sbuf_tensor("bq_pm_sb", [128, 8], F32))
    b_pm = e(nc.sbuf_tensor("b_pm_sb", [128, 32], F32))
    zrep_bf = e(nc.sbuf_tensor("zrep_bf", [128, 1024], BF16))
    eps_col = e(nc.sbuf_tensor("eps_col", [128, 1], F32))
    M_bf = e(nc.sbuf_tensor("M_bf", [128, 1024], BF16))
    A_bf = e(nc.sbuf_tensor("A_bf", [128, 1024], BF16))
    Bm_bf = e(nc.sbuf_tensor("Bm_bf", [32, D], BF16))
    LiM_bf = e(nc.sbuf_tensor("LiM_bf", [8, 32], BF16))
    LiMT_bf = e(nc.sbuf_tensor("LiMT_bf", [32, 8], BF16))
    Wsl_bf = e(nc.sbuf_tensor("Wsl_bf", [128, 256], BF16))

    stg = ExitStack()
    zrep_f = stg.enter_context(nc.sbuf_tensor("zrep_f", [128, 1024], F32))
    M_f = stg.enter_context(nc.sbuf_tensor("M_f", [128, 1024], F32))
    A_f = stg.enter_context(nc.sbuf_tensor("A_f", [128, 1024], F32))
    Bm_f = stg.enter_context(nc.sbuf_tensor("Bm_f", [32, D], F32))
    LiM_f = stg.enter_context(nc.sbuf_tensor("LiM_f", [8, 32], F32))
    LiMT_f = stg.enter_context(nc.sbuf_tensor("LiMT_f", [32, 8], F32))
    Wsl_f = stg.enter_context(nc.sbuf_tensor("Wsl_f", [128, 256], F32))

    def cdma(dst_ap, src_ap):
        return kb.dma("sync", (lambda g, d=dst_ap, s=src_ap: g.dma_start(out=d, in_=s)), sem="dc")

    dl = {}
    dl["bq"] = cdma(bq_pm[:, :], bq_d[:, :])
    dl["b"] = cdma(b_pm[:, :], b_d[:, :])
    dl["zrep"] = cdma(zrep_f[:, :], zrep_d[:, :])
    dl["M"] = cdma(M_f[:, :], M_d.ap())
    dl["A"] = cdma(A_f[:, :], A_d.ap())
    dl["Bm"] = cdma(Bm_f[:, :], Bm_d[:, :])
    dl["LiM"] = cdma(LiM_f[:, :], LiM_d[:, :])
    dl["LiMT"] = cdma(LiMT_f[:, :], LiMT_d[:, :])
    dl["Wsl"] = cdma(Wsl_f[:, :], Wsl_d[:, :])
    dl["ident"] = cdma(ident[:, :], id_d[:, :])

    all_consts = Step("sync", "dc", kb.count["dc"])
    for _k in dl:
        dl[_k] = all_consts

    c_ones1 = kb.op("vector", lambda v: v.memset(ones1_bf[:, :], 1.0))
    c_ones128 = kb.op("vector", lambda v: v.memset(ones128_bf[:, :], 1.0))
    c_id = all_consts
    c_zT = kb.op("vector", lambda v: v.tensor_copy(zrep_bf[:, :], zrep_f[:, :]), deps=[dl["zrep"]])
    c_eps = kb.op("vector", lambda v: v.memset(eps_col[:, :], EPS_DEN))
    c_M = kb.op("vector", lambda v: v.tensor_copy(M_bf[:, :], M_f[:, :]), deps=[dl["M"]])
    c_A = kb.op("vector", lambda v: v.tensor_copy(A_bf[:, :], A_f[:, :]), deps=[dl["A"]])
    c_Bm = kb.op("vector", lambda v: v.tensor_scalar(
        Bm_bf[:, :], Bm_f[:, :], 0.5, None, ALU.mult), deps=[dl["Bm"]])
    c_LiM = kb.op("vector", lambda v: v.tensor_copy(LiM_bf[:, :], LiM_f[:, :]), deps=[dl["LiM"]])
    c_LiMT = kb.op("vector", lambda v: v.tensor_copy(LiMT_bf[:, :], LiMT_f[:, :]), deps=[dl["LiMT"]])
    c_Wsl = kb.op("vector", lambda v: v.tensor_copy(Wsl_bf[:, :], Wsl_f[:, :]), deps=[dl["Wsl"]])
    stg.close()
    phase0 = [c_ones1, c_ones128, c_id, c_zT, c_M, c_A, c_Bm, c_LiM, c_LiMT, c_Wsl,
              dl["bq"], dl["b"]]

    stream_bf = e(nc.sbuf_tensor("stream_bf", [128, NI * T], BF16))
    qsel = e(nc.sbuf_tensor("qsel", [128, NH * T], BF16))
    Rsb = e(nc.sbuf_tensor("Rsb", [128, NH * T], BF16))
    wring = [e(nc.sbuf_tensor(f"wring{k}", [128, 2048], F32)) for k in range(2)]
    xring = [e(nc.sbuf_tensor(f"xring{k}", [128, 2048], F32)) for k in range(2)]
    stage = [e(nc.sbuf_tensor(f"stage{k}", [128, T], F32)) for k in range(2)]
    wbf = [e(nc.sbuf_tensor(f"wbf{k}", [128, 512], BF16)) for k in range(4)]
    sqbf = [e(nc.sbuf_tensor(f"sqbf{k}", [128, T], BF16)) for k in range(4)]

    m_r = e(nc.sbuf_tensor("m_r", [1, T], F32))
    r_r = e(nc.sbuf_tensor("r_r", [1, T], F32))
    e2_r = e(nc.sbuf_tensor("e2_r", [1, T], F32))
    v_r = e(nc.sbuf_tensor("v_r", [1, T], F32))
    row_bf = [e(nc.sbuf_tensor(f"row_bf{k}", [1, T], BF16)) for k in range(2)]
    mb = e(nc.sbuf_tensor("mb", [128, T], BF16))
    rb = e(nc.sbuf_tensor("rb", [128, T], BF16))
    denb = [e(nc.sbuf_tensor(f"denb{k}", [128, T], BF16)) for k in range(2)]

    z_sb = e(nc.sbuf_tensor("z_sb", [32, T], F32))
    zodd_bf = e(nc.sbuf_tensor("zodd_bf", [32, 512], BF16))
    Hs_f = e(nc.sbuf_tensor("Hs_f", [8, T], F32))
    zp_bf = e(nc.sbuf_tensor("zp_bf", [32, T], BF16))
    wfull_bf = e(nc.sbuf_tensor("wfull_bf", [8, T], BF16))
    gt = e(nc.sbuf_tensor("gt", [128, 320], F32))
    gc = e(nc.sbuf_tensor("gc", [128, 32], F32))
    sc_sb = e(nc.sbuf_tensor("sc_sb", [1, 64], F32))
    assert nc.sbuf_bytes_remaining >= 0, nc.sbuf_bytes_remaining

    effacc = qsel[:, 0:1024]
    effbf2 = [qsel[:, 4096 + k * 512: 4096 + (k + 1) * 512] for k in range(8)]

    G_HS, G_DS, G_LG, G_EX, G_PR, G_IM, G_MK, G_W, G_WN, G_T1 = range(10)

    def gts(k):
        return gt[:, k * 32:(k + 1) * 32]

    pb = [e(nc.psum_tensor(f"pb{k}", [128, 512], F32)) for k in range(8)]

    SB = lambda i: f"sb{i}"
    QS = lambda h: f"qs{h}"
    RS = lambda h: f"R{h}"
    PB = lambda k: f"pb{k}"
    X2D = lambda o: f"x2d{o}"

    first_sync = list(phase0)

    if debug:
        kb.dma("sync", lambda g: g.dma_start(out=taps["t_a0"].ap(), in_=A_bf[:, :]),
               deps=[c_A], sem="dot")

    for gidx in range(NI // 2):
        sl = gidx % 2
        i0 = gidx * 2
        xd = kb.dma("scalar", (lambda g, i0=i0, sl=sl: g.dma_start(
            out=xring[sl].ap().rearrange("p (a c) -> p a c", a=2),
            in_=xT_d[i0 * 128:(i0 + 2) * 128, :].rearrange("(a p) c -> p a c", p=128))),
            deps=(first_sync if gidx == 0 else ()), writes=(f"xr{sl}",), sem=f"dx{sl}")
        for k in range(2):
            i = i0 + k
            cv = kb.op("vector", (lambda v, i=i, sl=sl, k=k: v.tensor_copy(
                stream_bf[:, i * T:(i + 1) * T], xring[sl][:, k * T:(k + 1) * T])),
                deps=[xd], reads=(f"xr{sl}",), writes=(SB(i),))
            sq = kb.op("scalar", (lambda s, sl=sl, k=k, i=i: s.activation(
                sqbf[i % 4][:, :], xring[sl][:, k * T:(k + 1) * T], AF.Square)),
                deps=[xd], reads=(f"xr{sl}",), writes=(f"sq{i % 4}",))

            def mk_sum(t, i=i):
                for th in range(2):
                    inst = t.matmul(pb[4 + th][0:1, :], ones128_bf[:, :],
                                    stream_bf[:, i * T + th * 512: i * T + (th + 1) * 512],
                                    start=(i == 0), stop=(i == NI - 1), skip_group_check=True)
                return inst

            def mk_sq(t, i=i):
                for th in range(2):
                    inst = t.matmul(pb[6 + th][0:1, :], ones128_bf[:, :],
                                    sqbf[i % 4][:, th * 512:(th + 1) * 512],
                                    start=(i == 0), stop=(i == NI - 1), skip_group_check=True)
                return inst

            if i == 0:
                kb.op("tensor", mk_sum, deps=[c_ones128], reads=(SB(i),),
                      writes=(PB(4), PB(5)))
                kb.op("tensor", mk_sq, reads=(f"sq{i % 4}",), writes=(PB(6), PB(7)))
            else:
                s1 = kb.op("tensor", mk_sum, reads=(SB(i),), deps=kb.rdep(PB(4)))
                s2 = kb.op("tensor", mk_sq, reads=(f"sq{i % 4}",), deps=kb.rdep(PB(6)))
                kb.did_write(s1, PB(4), PB(5))
                kb.did_write(s2, PB(6), PB(7))

    def ln_rows():
        a = kb.op("vector", lambda v: v.tensor_scalar(
            m_r[0:1, 0:512], pb[4][0:1, :], 1.0 / D, None, ALU.mult),
            reads=(PB(4),), writes=("m_r",))
        b = kb.op("vector", lambda v: v.tensor_scalar(
            m_r[0:1, 512:1024], pb[5][0:1, :], 1.0 / D, None, ALU.mult),
            reads=(PB(5),), deps=kb.wdep("m_r"))
        kb.did_write(b, "m_r")
        kb.did_read(a, "m_r")
        c = kb.op("vector", lambda v: v.tensor_scalar(
            e2_r[0:1, 0:512], pb[6][0:1, :], 1.0 / D, None, ALU.mult),
            reads=(PB(6),), writes=("e2_r",))
        d = kb.op("vector", lambda v: v.tensor_scalar(
            e2_r[0:1, 512:1024], pb[7][0:1, :], 1.0 / D, None, ALU.mult),
            reads=(PB(7),), deps=kb.wdep("e2_r"))
        kb.did_write(d, "e2_r")
        kb.did_read(c, "e2_r")
        msq = kb.op("vector", lambda v: v.tensor_tensor(
            r_r[0:1, :], m_r[0:1, :], m_r[0:1, :], ALU.mult),
            deps=[a, b], reads=("m_r",), writes=("r_r",))
        var = kb.op("vector", lambda v: v.scalar_tensor_tensor(
            v_r[0:1, :], e2_r[0:1, :], EPS_LN, r_r[0:1, :], ALU.add, ALU.subtract),
            deps=[c, d, msq], reads=("e2_r", "r_r"), writes=("v_r",))
        sd = kb.op("scalar", lambda s: s.activation(
            e2_r[0:1, :], v_r[0:1, :], AF.Sqrt),
            reads=("v_r",), writes=("e2_r",))
        rr = kb.op("vector", lambda v: v.reciprocal(r_r[0:1, :], e2_r[0:1, :]),
                   reads=("e2_r",), writes=("r_r",))
        return [a, b], rr

    def bcast_row(rowname, row, dest, destname, bank0, bank1, rbf_idx, extra_deps=()):
        cvb = kb.op("vector", (lambda v: v.tensor_copy(row_bf[rbf_idx][0:1, :], row[0:1, :])),
                    deps=list(extra_deps), reads=(rowname,), writes=(f"rbf{rbf_idx}",))
        mm0 = kb.op("tensor", (lambda t: t.matmul(
            pb[bank0][:, :], ones1_bf[:, :], row_bf[rbf_idx][0:1, 0:512],
            start=True, stop=True, skip_group_check=True)),
            deps=[c_ones1, cvb], reads=(f"rbf{rbf_idx}",), writes=(PB(bank0),))
        mm1 = kb.op("tensor", (lambda t: t.matmul(
            pb[bank1][:, :], ones1_bf[:, :], row_bf[rbf_idx][0:1, 512:1024],
            start=True, stop=True, skip_group_check=True)),
            reads=(f"rbf{rbf_idx}",), writes=(PB(bank1),))
        ev = kb.op("vector", (lambda v: [
            v.tensor_copy(dest[:, 0:512], pb[bank0][:, :]),
            v.tensor_copy(dest[:, 512:1024], pb[bank1][:, :])][-1]),
            deps=[mm0, mm1], reads=(PB(bank0), PB(bank1)), writes=(destname,))
        return ev

    mA, rA = ln_rows()
    bc_m = bcast_row("m_r", m_r, mb, "mb", 0, 1, 0, extra_deps=mA)
    bc_r = bcast_row("r_r", r_r, rb, "rb", 2, 3, 1, extra_deps=[rA])

    normA = [None] * NI
    for i in range(NI):
        sl = i % 2
        engn = "vector" if (i % 2 == 0) else "gpsimd"
        t1 = kb.op(engn, (lambda v, i=i, sl=sl: v.tensor_tensor(
            stage[sl][:, :], stream_bf[:, i * T:(i + 1) * T], mb[:, :], ALU.subtract)),
            deps=[bc_m], reads=(SB(i), "mb"), writes=(f"st{sl}",))
        t2 = kb.op(engn, (lambda v, i=i, sl=sl: v.tensor_tensor(
            stream_bf[:, i * T:(i + 1) * T], stage[sl][:, :], rb[:, :], ALU.mult)),
            deps=[bc_r, t1], reads=(f"st{sl}", "rb"), writes=(SB(i),))
        normA[i] = t2

    if debug:
        kb.dma("sync", lambda g: g.dma_start(out=taps["t_m1"].ap(), in_=m_r[0:1, :]),
               reads=("m_r",), sem="dot")
        kb.dma("sync", lambda g: g.dma_start(out=taps["t_r1"].ap(), in_=r_r[0:1, :]),
               reads=("r_r",), sem="dot")
        tc = kb.op("vector", lambda v: v.tensor_copy(stage[0][:, :], stream_bf[:, 0:T]),
                   reads=(SB(0),), writes=("st0",))
        kb.dma("sync", lambda g: g.dma_start(out=taps["t_xn1"].ap(), in_=stage[0][:, :]),
               deps=[tc], reads=("st0",), sem="dot")

    R_done = [None] * NH

    def emit_head(h, B0c):
        p2 = h % 2
        dest = mb if p2 == 0 else rb
        dn_ = "mb" if p2 == 0 else "rb"
        dbn = f"denb{p2}"
        mmn = kb.op("tensor", (lambda t, h=h, B0=B0c: [
            t.matmul(pb[B0][:, :], M_bf[:, h * 128:(h + 1) * 128],
                     qsel[:, h * T: h * T + 512], start=True, stop=True, skip_group_check=True),
            t.matmul(pb[B0 + 1][:, :], M_bf[:, h * 128:(h + 1) * 128],
                     qsel[:, h * T + 512: h * T + 1024], start=True, stop=True, skip_group_check=True)][-1]),
            deps=[c_M], reads=(QS(h),), writes=(PB(B0c), PB(B0c + 1)))
        mmd = kb.op("tensor", (lambda t, h=h, B0=B0c: [
            t.matmul(pb[B0 + 2][:, :], zrep_bf[:, h * 128:(h + 1) * 128],
                     qsel[:, h * T: h * T + 512], start=True, stop=True, skip_group_check=True),
            t.matmul(pb[B0 + 3][:, :], zrep_bf[:, h * 128:(h + 1) * 128],
                     qsel[:, h * T + 512: h * T + 1024], start=True, stop=True, skip_group_check=True)][-1]),
            deps=[c_zT], reads=(QS(h),), writes=(PB(B0c + 2), PB(B0c + 3)))
        ep = kb.op("scalar", (lambda s, B0=B0c, p2=p2: [
            s.activation(denb[p2][:, 0:512], pb[B0 + 2][:, :], AF.Identity, bias=eps_col[:, 0:1]),
            s.activation(denb[p2][:, 512:1024], pb[B0 + 3][:, :], AF.Identity, bias=eps_col[:, 0:1])][-1]),
            deps=[mmd, c_eps], reads=(PB(B0c + 2), PB(B0c + 3)), writes=(dbn,))
        def _rc(v, p2=p2, dest=dest):
            with v.bass.allow_low_precision("rden in bf16 == prior bf16 broadcast path"):
                return v.reciprocal(dest[:, :], denb[p2][:, :])
        rc = kb.op("vector", _rc, deps=[ep], reads=(dbn,), writes=(dn_,))
        rr1 = kb.op("vector", (lambda v, h=h, B0=B0c, dest=dest: v.tensor_tensor(
            Rsb[:, h * T: h * T + 512], pb[B0][:, :], dest[:, 0:512], ALU.mult)),
            deps=[rc, mmn], reads=(PB(B0c), dn_), writes=(RS(h),))
        rr2 = kb.op("vector", (lambda v, h=h, B0=B0c, dest=dest: v.tensor_tensor(
            Rsb[:, h * T + 512: h * T + 1024], pb[B0 + 1][:, :], dest[:, 512:1024], ALU.mult)),
            deps=[rc, mmn], reads=(PB(B0c + 1), dn_))
        kb.did_write(rr2, RS(h))
        kb.did_read(rr1, RS(h))
        R_done[h] = rr2

    for s_ in range(4):
        B0 = (s_ % 2) * 4
        mm_last = None
        for g in range(4):
            sl = g % 2
            wd = kb.dma("sync", (lambda y, s_=s_, g=g, sl=sl: y.dma_start(
                out=wring[sl].ap().rearrange("p (a c) -> p a c", a=8),
                in_=wq_d[s_, 8 * g:8 * g + 8, :, :].rearrange("a p c -> p a c"))),
                writes=(f"wr{sl}",), sem=f"dw{sl}")
            for k in range(8):
                i = 8 * g + k
                ws = i % 4
                wc = kb.op("scalar", (lambda s, sl=sl, k=k, ws=ws: s.activation(
                    wbf[ws][:, 0:256], wring[sl][:, k * 256:(k + 1) * 256], AF.Identity)),
                    deps=[wd], reads=(f"wr{sl}",), writes=(f"wbf{ws}",))

                def mkq(t, i=i, ws=ws, B0=B0):
                    for oc in range(2):
                        for th in range(2):
                            inst = t.matmul(
                                pb[B0 + oc * 2 + th][:, :],
                                wbf[ws][:, oc * 128:(oc + 1) * 128],
                                stream_bf[:, i * T + th * 512: i * T + (th + 1) * 512],
                                start=(i == 0), stop=(i == NI - 1), skip_group_check=True)
                    return inst

                if i == 0:
                    mm_last = kb.op("tensor", mkq, deps=[wc, normA[i]],
                                    reads=(f"wbf{ws}", SB(i)),
                                    writes=tuple(PB(B0 + kk) for kk in range(4)))
                else:
                    mm_last = kb.op("tensor", mkq, deps=[wc, normA[i]] + kb.rdep(PB(B0)),
                                    reads=(f"wbf{ws}", SB(i)))
                    kb.did_write(mm_last, *[PB(B0 + kk) for kk in range(4)])
        for oc in range(2):
            h = s_ * 2 + oc
            for th in range(2):
                bank = B0 + oc * 2 + th
                lohi = kb.op("vector", (lambda v, bank=bank, h=h: [
                    v.tensor_scalar(stage[0][:, 0:512], pb[bank][:, :],
                                    bq_pm[:, h:h + 1], 0.0, ALU.add, ALU.min),
                    v.tensor_scalar(stage[0][:, 512:1024], pb[bank][:, :],
                                    bq_pm[:, h:h + 1], 0.0, ALU.add, ALU.max)][-1]),
                    deps=[dl["bq"]], reads=(PB(bank),), writes=("st0",))
                ex = kb.op("scalar", (lambda s: s.activation(
                    stage[1][:, 0:512], stage[0][:, 0:512], AF.Exp)),
                    deps=[lohi], reads=("st0",), writes=("st1",))
                qs = kb.op("vector", (lambda v, h=h, th=th: v.tensor_tensor(
                    qsel[:, h * T + th * 512: h * T + (th + 1) * 512],
                    stage[1][:, 0:512], stage[0][:, 512:1024], ALU.add)),
                    deps=[ex], reads=("st0", "st1"), writes=(QS(h),))
                kb.did_read(qs, QS(h))
        emit_head(s_ * 2, B0)
        emit_head(s_ * 2 + 1, B0)

    if debug:
        kb.dma("sync", lambda g: g.dma_start(out=taps["t_a1"].ap(), in_=A_bf[:, :]),
               deps=kb.rdep(QS(7)), sem="dot")
        tq = kb.op("vector", lambda v: v.tensor_copy(stage[1][:, :], qsel[:, 0:T]),
                   reads=(QS(0),), writes=("st1",))
        kb.dma("sync", lambda g: g.dma_start(out=taps["t_qs"].ap(), in_=stage[1][:, :]),
               deps=[tq], reads=("st1",), sem="dot")

    if debug:
        tr = kb.op("vector", lambda v: v.tensor_copy(stage[0][:, :], Rsb[:, 0:T]),
                   reads=(RS(0),), writes=("st0",))
        kb.dma("sync", lambda g: g.dma_start(out=taps["t_R"].ap(), in_=stage[0][:, :]),
               deps=[tr], reads=("st0",), sem="dot")

    pend_stats = []

    def emit_stats(slot):
        (o, cb, sq) = slot

        def mk_sum2(t, o=o):
            for th in range(2):
                inst = t.matmul(pb[4 + th][0:1, :], ones128_bf[:, :],
                                stream_bf[:, o * T + th * 512: o * T + (th + 1) * 512],
                                start=(o == 0), stop=(o == NI - 1), skip_group_check=True)
            return inst

        def mk_sq2(t, o=o):
            for th in range(2):
                inst = t.matmul(pb[6 + th][0:1, :], ones128_bf[:, :],
                                sqbf[o % 4][:, th * 512:(th + 1) * 512],
                                start=(o == 0), stop=(o == NI - 1), skip_group_check=True)
            return inst

        if o == 0:
            kb.op("tensor", mk_sum2, deps=[cb], reads=(SB(o),), writes=(PB(4), PB(5)))
            kb.op("tensor", mk_sq2, deps=[sq], reads=(f"sq{o % 4}",), writes=(PB(6), PB(7)))
        else:
            s1 = kb.op("tensor", mk_sum2, deps=[cb] + kb.rdep(PB(4)), reads=(SB(o),))
            s2 = kb.op("tensor", mk_sq2, deps=[sq] + kb.rdep(PB(6)), reads=(f"sq{o % 4}",))
            kb.did_write(s1, PB(4), PB(5))
            kb.did_write(s2, PB(6), PB(7))

    def emit_eff_group(s_, g):
        sl = (s_ * 4 + g) % 2
        eb = (s_ * 4 + g) % 8
        wd = kb.dma("sync", (lambda y, s_=s_, g=g, sl=sl: y.dma_start(
            out=wring[sl].ap().rearrange("p (a c) -> p a c", a=8),
            in_=wo_d[s_, 2 * g:2 * g + 2, :, :, :].rearrange("h j p c -> p (h j) c"))),
            writes=(f"wr{sl}",), sem=f"dw{sl}")
        es1 = kb.op("vector", (lambda v, sl=sl: v.tensor_tensor(
            effacc.rearrange("p (h c) -> p h c", h=2),
            wring[sl].ap().rearrange("p (h u c) -> p h u c", h=2, u=2)[:, :, 0, :],
            wring[sl].ap().rearrange("p (h u c) -> p h u c", h=2, u=2)[:, :, 1, :],
            ALU.add)),
            deps=[wd], reads=(f"wr{sl}",), writes=("effacc",))
        es2 = kb.op("vector", (lambda v, eb=eb: v.tensor_tensor(
            effbf2[eb].rearrange("p (h c) -> p h c", h=2),
            effacc.rearrange("p (h u c) -> p h u c", h=2, u=2)[:, :, 0, :],
            effacc.rearrange("p (h u c) -> p h u c", h=2, u=2)[:, :, 1, :],
            ALU.add)),
            deps=[es1], reads=("effacc",), writes=(f"eff{eb}",))
        return es2

    eff_ready = {}
    for g in range(4):
        eff_ready[(0, g)] = emit_eff_group(0, g)

    for s_ in range(16):
        new_stats = []
        mm_o = None
        for g in range(4):
            eb = (s_ * 4 + g) % 8
            es2 = eff_ready.pop((s_, g))
            for hh in range(2):
                h = 2 * g + hh

                def mko(t, h=h, hh=hh, eb=eb):
                    for oc in range(2):
                        for th in range(2):
                            inst = t.matmul(
                                pb[oc * 2 + th][:, :],
                                effbf2[eb][:, hh * 256 + oc * 128: hh * 256 + (oc + 1) * 128],
                                Rsb[:, h * T + th * 512: h * T + (th + 1) * 512],
                                start=(h == 0), stop=(h == NH - 1), skip_group_check=True)
                    return inst

                if h == 0:
                    mm_o = kb.op("tensor", mko, deps=[es2],
                                 reads=(f"eff{eb}", RS(h)),
                                 writes=(PB(0), PB(1), PB(2), PB(3)))
                else:
                    mm_o = kb.op("tensor", mko, deps=[es2] + kb.rdep(PB(0)),
                                 reads=(f"eff{eb}", RS(h)))
                    kb.did_write(mm_o, PB(0), PB(1), PB(2), PB(3))
            if s_ < 15:
                eff_ready[(s_ + 1, g)] = emit_eff_group(s_ + 1, g)
        o0 = s_ * 2
        slx = s_ % 2
        xd2 = kb.dma("scalar", (lambda y, o0=o0, slx=slx: y.dma_start(
            out=xring[slx].ap().rearrange("p (a c) -> p a c", a=2),
            in_=xT_d[o0 * 128:(o0 + 2) * 128, :].rearrange("(a p) c -> p a c", p=128))),
            writes=(f"xr{slx}",), sem=f"dx{slx}")
        for oc in range(2):
            o = s_ * 2 + oc
            v1 = kb.op("vector", (lambda v, oc=oc, slx=slx: [
                v.tensor_tensor(stage[oc][:, 0:512], pb[oc * 2][:, :],
                                xring[slx][:, oc * T:oc * T + 512], ALU.add),
                v.tensor_tensor(stage[oc][:, 512:1024], pb[oc * 2 + 1][:, :],
                                xring[slx][:, oc * T + 512:(oc + 1) * T], ALU.add)][-1]),
                deps=[mm_o, xd2], reads=(PB(oc * 2), PB(oc * 2 + 1), f"xr{slx}"),
                writes=(f"st{oc}",))
            kb.dma("scalar", (lambda y, o=o, oc=oc: y.dma_start(
                out=x2_d[o * 128:(o + 1) * 128, :], in_=stage[oc][:, :])),
                deps=[v1], reads=(f"st{oc}",), writes=(X2D(o),), sem=f"dy{oc}")
            cb = kb.op("gpsimd", (lambda g, o=o, oc=oc: g.tensor_copy(
                stream_bf[:, o * T:(o + 1) * T], stage[oc][:, :])),
                deps=[v1], reads=(f"st{oc}",), writes=(SB(o),))
            sq = kb.op("scalar", (lambda s, oc=oc, o=o: s.activation(
                sqbf[o % 4][:, :], stage[oc][:, :], AF.Square)),
                deps=[v1], reads=(f"st{oc}",), writes=(f"sq{o % 4}",))
            new_stats.append((o, cb, sq))
        for slot in pend_stats:
            emit_stats(slot)
        pend_stats = new_stats
    for slot in pend_stats:
        emit_stats(slot)
    x2_all_written = [Step("sync", "dy0", kb.count["dy0"]),
                      Step("sync", "dy1", kb.count["dy1"])]

    mB, rB = ln_rows()
    if debug:
        kb.dma("sync", lambda g: g.dma_start(out=taps["t_a2"].ap(), in_=A_bf[:, :]),
               deps=mB, sem="dot")
    bc_m2 = bcast_row("m_r", m_r, mb, "mb", 4, 5, 0, extra_deps=mB)
    bc_r2 = bcast_row("r_r", r_r, rb, "rb", 6, 7, 1, extra_deps=[rB])

    normB = [None] * NI
    for i in range(NI):
        sl = i % 2
        engn = "vector" if (i % 2 == 0) else "gpsimd"
        t1 = kb.op(engn, (lambda v, i=i, sl=sl: v.tensor_tensor(
            stage[sl][:, :], stream_bf[:, i * T:(i + 1) * T], mb[:, :], ALU.subtract)),
            deps=[bc_m2], reads=(SB(i), "mb"), writes=(f"st{sl}",))
        t2 = kb.op(engn, (lambda v, i=i, sl=sl: v.tensor_tensor(
            stream_bf[:, i * T:(i + 1) * T], stage[sl][:, :], rb[:, :], ALU.mult)),
            deps=[bc_r2, t1], reads=(f"st{sl}", "rb"), writes=(SB(i),))
        normB[i] = t2

    if debug:
        kb.dma("sync", lambda g: g.dma_start(out=taps["t_m2"].ap(), in_=m_r[0:1, :]),
               reads=("m_r",), sem="dot")
        kb.dma("sync", lambda g: g.dma_start(out=taps["t_r2"].ap(), in_=r_r[0:1, :]),
               reads=("r_r",), sem="dot")
        tn2 = kb.op("vector", lambda v: v.tensor_copy(stage[0][:, :], stream_bf[:, 0:T]),
                    reads=(SB(0),), writes=("st0",))
        kb.dma("sync", lambda g: g.dma_start(out=taps["t_xn2"].ap(), in_=stage[0][:, :]),
               deps=[tn2], reads=("st0",), sem="dot")

    for i in range(NI):
        def mkz(t, i=i):
            for th in range(2):
                inst = t.matmul(pb[4 + th][0:32, :], A_bf[:, i * 32:(i + 1) * 32],
                                stream_bf[:, i * T + th * 512: i * T + (th + 1) * 512],
                                start=(i == 0), stop=(i == NI - 1), skip_group_check=True)
            return inst

        if i == 0:
            kb.op("tensor", mkz, deps=[normB[i], c_A], reads=(SB(i),),
                  writes=(PB(4), PB(5)))
        else:
            sz = kb.op("tensor", mkz, deps=[normB[i]] + kb.rdep(PB(4)), reads=(SB(i),))
            kb.did_write(sz, PB(4), PB(5))
    zev = kb.op("vector", (lambda v: [
        v.tensor_copy(z_sb[:, 0:512], pb[4][0:32, :]),
        v.tensor_copy(z_sb[:, 512:1024], pb[5][0:32, :])][-1]),
        reads=(PB(4), PB(5)), writes=("z_sb",))

    for i in range(NI):
        def mkh(t, i=i):
            for th in range(2):
                inst = t.matmul(pb[6 + th][0:8, :], Wsl_bf[:, i * 8:(i + 1) * 8],
                                stream_bf[:, i * T + th * 512: i * T + (th + 1) * 512],
                                start=(i == 0), stop=(i == NI - 1), skip_group_check=True)
            return inst

        if i == 0:
            kb.op("tensor", mkh, deps=[normB[i], c_Wsl], reads=(SB(i),),
                  writes=(PB(6), PB(7)))
        else:
            shh = kb.op("tensor", mkh, deps=[normB[i]] + kb.rdep(PB(6)), reads=(SB(i),))
            kb.did_write(shh, PB(6), PB(7))
    hev = kb.op("vector", (lambda v: [
        v.tensor_scalar(Hs_f[0:8, 0:512], pb[6][0:8, :], b_pm[0:8, 0:1], None, ALU.add),
        v.tensor_scalar(Hs_f[0:8, 512:1024], pb[7][0:8, :], b_pm[0:8, 0:1], None, ALU.add)][-1]),
        deps=[dl["b"]], reads=(PB(6), PB(7)), writes=("Hs",))

    if debug:
        kb.dma("sync", lambda g: g.dma_start(out=taps["t_hs"].ap(), in_=Hs_f[:, :]),
               reads=("Hs",), sem="dot")
        kb.dma("sync", lambda g: g.dma_start(out=taps["t_z"].ap(), in_=z_sb[:, :]),
               reads=("z_sb",), sem="dot")

    zo = kb.op("vector", (lambda v: v.tensor_copy(zodd_bf[:, :], z_sb[:, 1:1024:2])),
               reads=("z_sb",), writes=("zodd",))
    for tb in range(4):
        mmds = kb.op("tensor", (lambda t, tb=tb: t.matmul(
            pb[4 + tb][:, 0:8], zodd_bf[:, tb * 128:(tb + 1) * 128], LiMT_bf[:, :],
            start=True, stop=True, skip_group_check=True)),
            deps=[zo, c_LiMT], reads=("zodd",), writes=(PB(4 + tb),))
        kb.op("vector", (lambda v, tb=tb: v.tensor_copy(
            gts(G_DS)[:, tb * 8:(tb + 1) * 8], pb[4 + tb][:, 0:8])),
            deps=[mmds], reads=(PB(4 + tb),), writes=(f"gds{tb}",))

    for tb in range(4):
        mmhT = kb.op("tensor", (lambda t, tb=tb: t.transpose(
            pb[4 + tb][:, 0:8], Hs_f[0:8, tb * 256 + 1:(tb + 1) * 256:2], ident[0:8, 0:8])),
            deps=[c_id], reads=("Hs",), writes=(PB(4 + tb),))
        kb.op("vector", (lambda v, tb=tb: v.tensor_copy(
            gts(G_HS)[:, tb * 8:(tb + 1) * 8], pb[4 + tb][:, 0:8])),
            deps=[mmhT], reads=(PB(4 + tb),), writes=(f"ghs{tb}",))

    ghs_all = tuple(f"ghs{tb}" for tb in range(4))
    gds_all = tuple(f"gds{tb}" for tb in range(4))

    if debug:
        kb.dma("sync", lambda g: g.dma_start(out=taps["t_gt2"].ap(), in_=gt[:, 0:64]),
               reads=ghs_all + gds_all, sem="dot")
        tzo = kb.op("vector", lambda v: v.tensor_copy(stage[0][0:32, 0:512], zodd_bf[:, :]),
                    reads=("zodd",), writes=("st0",))
        kb.dma("sync", lambda g: g.dma_start(out=taps["t_zo"].ap(), in_=stage[0][0:32, 0:512]),
               deps=[tzo], reads=("st0",), sem="dot")

    zrest = kb.op("vector", (lambda v: v.memset(sc_sb[0:1, 2:64], 0.0)), writes=("sc",))
    redm = []
    for (srcs, col, which) in ((ghs_all, 0, G_HS), (gds_all, 1, G_DS)):
        r1 = kb.op("vector", (lambda v, col=col, which=which: v.reduce_max(
            gc[:, col:col + 1], gts(which)[:, :], AX.X, apply_absolute_value=True)),
            reads=srcs, writes=(f"gcmax{col}",))
        mmt = kb.op("tensor", (lambda t, col=col: t.transpose(
            pb[4 + col][0:1, 0:128], gc[:, col:col + 1], ident[:, :])),
            deps=[r1, c_id], reads=(f"gcmax{col}",), writes=(PB(4 + col),))
        r2 = kb.op("vector", (lambda v, col=col: v.reduce_max(
            sc_sb[0:1, col:col + 1], pb[4 + col][0:1, 0:128], AX.X,
            apply_absolute_value=True)), deps=[mmt, zrest], reads=(PB(4 + col),))
        kb.did_write(r2, "sc")
        kb.did_read(zrest, "sc")
        redm.append(r2)

    ccw = kb.dma("gpsimd", (lambda g: g.dma_start(out=ccb_d[:, :], in_=sc_sb[0:1, :])),
                 deps=redm, reads=("sc",), sem="dp")
    cc_in = bass.AP(ccb_d.ap().tensor, 0, [[1, 1], [1, 64]])
    cc_out = bass.AP(ccb_d.ap().tensor, 0, [[1, 1], [1, 64]])
    ccr = kb.op("gpsimd", (lambda g: g.collective_compute(
        "AllReduce", ALU.max, replica_groups=[list(range(NCORES))],
        ins=[cc_in], outs=[cc_out])), deps=[ccw], sem="cc")
    ccb = kb.dma("gpsimd", (lambda g: g.dma_start(out=sc_sb[0:1, :], in_=ccb_d[:, :])),
                 deps=[ccr], writes=("sc",), sem="dp")

    post_state = {}

    def emit_post_collective():
        sc1 = kb.op("vector", (lambda v: v.tensor_scalar(
            sc_sb[0:1, 32:34], sc_sb[0:1, 0:2], EPS_DEN, None, ALU.max)),
            deps=[ccb], reads=("sc",))
        kb.did_write(sc1, "sc")
        kb.did_read(ccb, "sc")
        sc2 = kb.op("vector", (lambda v: v.reciprocal(sc_sb[0:1, 34:36], sc_sb[0:1, 32:34])),
                    deps=[sc1])
        sc3a = kb.op("vector", (lambda v: v.tensor_scalar(
            sc_sb[0:1, 36:38], sc_sb[0:1, 34:36], 0.5, None, ALU.mult)), deps=[sc2])
        sc3 = kb.op("vector", (lambda v: v.tensor_copy(
            row_bf[0][0:1, 0:2], sc_sb[0:1, 36:38])), deps=[sc3a], writes=("rbf0",))
        mmsc = kb.op("tensor", (lambda t: t.matmul(
            pb[4][:, 0:512], ones1_bf[:, :], row_bf[0][0:1, 0:512], start=True, stop=True,
            skip_group_check=True)), deps=[sc3, c_ones1], reads=("rbf0",), writes=(PB(4),))
        scev = kb.op("vector", (lambda v: v.tensor_copy(gc[:, 2:4], pb[4][:, 0:2])),
                     deps=[mmsc], reads=(PB(4),), writes=("gsc",))
        if debug:
            kb.dma("sync", lambda g: g.dma_start(out=taps["t_scb2"].ap(), in_=gc[:, 0:4]),
                   deps=[scev], reads=("gsc",), sem="dot")

        if debug:
            kb.dma("sync", lambda g: g.dma_start(out=taps["t_sc"].ap(), in_=sc_sb[0:1, :]),
                   deps=[sc3], reads=("sc",), sem="dot")

        def vop(fn, deps=(), reads=(), writes=()):
            return kb.op("vector", fn, deps, reads, writes)

        g1 = vop(lambda v: v.tensor_scalar(
            gts(G_T1)[:, :], gts(G_HS)[:, :], gc[:, 2:3], None, ALU.mult),
            reads=ghs_all + ("gsc",), writes=("gT1",))
        g2 = vop(lambda v: v.scalar_tensor_tensor(
            gts(G_LG)[:, :], gts(G_DS)[:, :], gc[:, 3:4], gts(G_T1)[:, :],
            ALU.mult, ALU.add),
            deps=[g1], reads=gds_all + ("gsc", "gT1"), writes=("gLG",))
        mx = vop(lambda v: [v.reduce_max(gc[:, 4 + tb:5 + tb],
                                         gts(G_LG)[:, tb * 8:(tb + 1) * 8], AX.X)
                            for tb in range(4)][-1],
                 deps=[g2], reads=("gLG",), writes=("gmx",))
        sh = vop(lambda v: [v.tensor_scalar(
            gts(G_T1)[:, tb * 8:(tb + 1) * 8], gts(G_LG)[:, tb * 8:(tb + 1) * 8],
            gc[:, 4 + tb:5 + tb], None, ALU.subtract) for tb in range(4)][-1],
            deps=[mx, g2], reads=("gLG", "gmx"), writes=("gT1",))
        gex = kb.op("scalar", (lambda s: s.activation(
            gts(G_EX)[:, :], gts(G_T1)[:, :], AF.Exp)), deps=[sh],
            reads=("gT1",), writes=("gEX",))
        sm = vop(lambda v: [v.reduce_sum(gc[:, 8 + tb:9 + tb],
                                         gts(G_EX)[:, tb * 8:(tb + 1) * 8], AX.X)
                            for tb in range(4)][-1],
                 deps=[gex], reads=("gEX",), writes=("gsm",))
        grc = vop(lambda v: v.reciprocal(gc[:, 12:16], gc[:, 8:12]),
                  deps=[sm], reads=("gsm",), writes=("grs",))
        pr = vop(lambda v: [v.tensor_scalar(
            gts(G_PR)[:, tb * 8:(tb + 1) * 8], gts(G_EX)[:, tb * 8:(tb + 1) * 8],
            gc[:, 12 + tb:13 + tb], None, ALU.mult) for tb in range(4)][-1],
            deps=[grc], reads=("gEX", "grs"), writes=("gPR",))
        m1 = vop(lambda v: [v.reduce_max(gc[:, 16 + tb:17 + tb],
                                         gts(G_PR)[:, tb * 8:(tb + 1) * 8], AX.X)
                            for tb in range(4)][-1],
                 deps=[pr], reads=("gPR",), writes=("gm1",))
        im = vop(lambda v: [v.tensor_scalar(
            gts(G_IM)[:, tb * 8:(tb + 1) * 8], gts(G_PR)[:, tb * 8:(tb + 1) * 8],
            gc[:, 16 + tb:17 + tb], None, ALU.is_ge) for tb in range(4)][-1],
            deps=[m1], reads=("gPR", "gm1"), writes=("gIM",))
        cn = vop(lambda v: [v.reduce_sum(gc[:, 20 + tb:21 + tb],
                                         gts(G_IM)[:, tb * 8:(tb + 1) * 8], AX.X)
                            for tb in range(4)][-1],
                 deps=[im], reads=("gIM",), writes=("gcn",))
        mk = vop(lambda v: v.scalar_tensor_tensor(
            gts(G_MK)[:, :], gts(G_IM)[:, :], -1e30, gts(G_PR)[:, :],
            ALU.mult, ALU.add),
            deps=[im], reads=("gIM", "gPR"), writes=("gMK",))
        m2 = vop(lambda v: [v.reduce_max(gc[:, 24 + tb:25 + tb],
                                         gts(G_MK)[:, tb * 8:(tb + 1) * 8], AX.X)
                            for tb in range(4)][-1],
                 deps=[mk], reads=("gMK",), writes=("gm2",))
        fl = vop(lambda v: v.tensor_scalar(gc[:, 28:32], gc[:, 20:24], 1.5, None, ALU.is_ge),
                 deps=[cn], reads=("gcn",), writes=("gfl",))
        d12 = vop(lambda v: v.tensor_tensor(gc[:, 16:20], gc[:, 16:20], gc[:, 24:28], ALU.subtract),
                  deps=[m2], reads=("gm1", "gm2"), writes=("gm1",))
        fd = vop(lambda v: v.tensor_tensor(gc[:, 28:32], gc[:, 28:32], gc[:, 16:20], ALU.mult),
                 deps=[fl, d12], reads=("gfl", "gm1"), writes=("gfl",))
        thr = vop(lambda v: v.tensor_tensor(gc[:, 24:28], gc[:, 24:28], gc[:, 28:32], ALU.add),
                  deps=[fd], reads=("gm2", "gfl"), writes=("gm2",))
        pm = vop(lambda v: [v.tensor_scalar(
            gts(G_T1)[:, tb * 8:(tb + 1) * 8], gts(G_PR)[:, tb * 8:(tb + 1) * 8],
            gc[:, 24 + tb:25 + tb], 2.0, ALU.subtract, ALU.mult) for tb in range(4)][-1],
            deps=[thr, gex], reads=("gPR", "gm2"), writes=("gT1",))
        sg = kb.op("scalar", (lambda s: s.activation(
            gts(G_MK)[:, :], gts(G_T1)[:, :], AF.Sigmoid)),
            deps=[pm, m2], reads=("gT1",), writes=("gMK",))
        w_ = vop(lambda v: v.tensor_tensor(gts(G_W)[:, :], gts(G_PR)[:, :], gts(G_MK)[:, :], ALU.mult),
                 deps=[sg], reads=("gPR", "gMK"), writes=("gW",))
        ws = vop(lambda v: [v.reduce_sum(gc[:, 16 + tb:17 + tb],
                                         gts(G_W)[:, tb * 8:(tb + 1) * 8], AX.X)
                            for tb in range(4)][-1],
                 deps=[w_, d12, fd], reads=("gW",), writes=("gm1",))
        wsp = vop(lambda v: v.tensor_scalar(gc[:, 20:24], gc[:, 16:20], 1e-9, None, ALU.add),
                  deps=[ws], reads=("gm1",), writes=("gcn",))
        wrc = vop(lambda v: v.reciprocal(gc[:, 24:28], gc[:, 20:24]),
                  deps=[wsp, thr], reads=("gcn",), writes=("gm2",))
        wn = vop(lambda v: [v.tensor_scalar(
            gts(G_WN)[:, tb * 8:(tb + 1) * 8], gts(G_W)[:, tb * 8:(tb + 1) * 8],
            gc[:, 24 + tb:25 + tb], None, ALU.mult) for tb in range(4)][-1],
            deps=[wrc], reads=("gW", "gm2"), writes=("gWN",))

        if debug:
            tg = vop(lambda v: [
                v.tensor_copy(gt[:, 64:96], gts(G_PR)[:, :]),
                v.tensor_copy(gt[:, 96:128], gts(G_WN)[:, :])][-1],
                deps=[wn], reads=("gPR", "gWN"), writes=("gLG", "gEX"))
            kb.dma("sync", lambda g: g.dma_start(out=taps["t_gt"].ap(), in_=gt[:, 64:128]),
                   deps=[tg], sem="dot")

        wf_last = None
        for tb in range(4):
            mmw = kb.op("tensor", (lambda t, tb=tb: t.transpose(
                pb[5][0:8, 0:128], gts(G_WN)[:, tb * 8:(tb + 1) * 8], ident[:, :])),
                deps=[wn, c_id], reads=("gWN",), writes=(PB(5),))
            wf_last = kb.op("vector", (lambda v, tb=tb: [
                v.tensor_copy(wfull_bf[0:8, tb * 256 + 0:(tb + 1) * 256:2], pb[5][0:8, 0:128]),
                v.tensor_copy(wfull_bf[0:8, tb * 256 + 1:(tb + 1) * 256:2], pb[5][0:8, 0:128])][-1]),
                deps=[mmw], reads=(PB(5),))
            if tb == 0:
                kb.did_write(wf_last, "wfull")
            else:
                kb.did_read(wf_last, "wfull")

        mmp = kb.op("tensor", (lambda t: [
            t.matmul(pb[4][0:32, :], LiM_bf[:, :], wfull_bf[0:8, 0:512],
                     start=True, stop=True, skip_group_check=True),
            t.matmul(pb[5][0:32, :], LiM_bf[:, :], wfull_bf[0:8, 512:1024],
                     start=True, stop=True, skip_group_check=True)][-1]),
            deps=[c_LiM] + kb.bufs["wfull"]["pend"] + [kb.bufs["wfull"]["w"]],
            writes=(PB(4), PB(5)))
        zp_done = kb.op("vector", (lambda v: [
            v.tensor_tensor(zp_bf[:, 0:512], z_sb[:, 0:512], pb[4][0:32, :], ALU.mult),
            v.tensor_tensor(zp_bf[:, 512:1024], z_sb[:, 512:1024], pb[5][0:32, :], ALU.mult)][-1]),
            deps=[mmp], reads=(PB(4), PB(5), "z_sb"), writes=("zp",))

        if debug:
            tzp = vop(lambda v: v.tensor_copy(stage[0][0:32, :], zp_bf[:, :]),
                      reads=("zp",), writes=("st0",))
            kb.dma("sync", lambda g: g.dma_start(out=taps["t_zp"].ap(), in_=stage[0][0:32, :]),
                   deps=[tzp], reads=("st0",), sem="dot")
            twf = vop(lambda v: v.tensor_copy(stage[1][0:8, :], wfull_bf[:, :]),
                      deps=[wf_last], writes=("st1",))
            kb.dma("sync", lambda g: g.dma_start(out=taps["t_wf"].ap(), in_=stage[1][0:8, :]),
                   deps=[twf], reads=("st1",), sem="dot")


        post_state["zp_done"] = zp_done

    out_steps = []
    fstage = [stage[0], stage[1],
              Rsb[:, 0:2048].bitcast(F32), Rsb[:, 2048:4096].bitcast(F32)]
    pend_ow = []
    for s_ in range(16):
        B0 = 0 if (s_ < 2 or s_ % 2 == 0) else 4
        o0 = s_ * 2
        slx = s_ % 2
        x2rd = kb.dma("scalar", (lambda y, o0=o0, slx=slx: y.dma_start(
            out=xring[slx].ap().rearrange("p (a c) -> p a c", a=2),
            in_=x2_d[o0 * 128:(o0 + 2) * 128, :].rearrange("(a p) c -> p a c", p=128))),
            deps=[x2_all_written], writes=(f"xr{slx}",), sem=f"dx{slx}")
        for owfn in pend_ow:
            out_steps.append(owfn())
        pend_ow = []
        mm_f = None
        for g in range(4):
            sl = g % 2
            wd = kb.dma("sync", (lambda y, s_=s_, g=g, sl=sl: y.dma_start(
                out=wring[sl].ap().rearrange("p (a c) -> p a c", a=8),
                in_=wt_d[s_, 8 * g:8 * g + 8, :, :].rearrange("a p c -> p a c"))),
                writes=(f"wr{sl}",), sem=f"dw{sl}")
            for k in range(8):
                i = 8 * g + k
                ws = i % 4
                wc = kb.op("scalar", (lambda s, sl=sl, k=k, ws=ws: s.activation(
                    wbf[ws][:, 0:256], wring[sl][:, k * 256:(k + 1) * 256], AF.Identity)),
                    deps=[wd], reads=(f"wr{sl}",), writes=(f"wbf{ws}",))

                def mkf(t, i=i, ws=ws, B0=B0):
                    for oc in range(2):
                        for th in range(2):
                            inst = t.matmul(
                                pb[B0 + oc * 2 + th][:, :],
                                wbf[ws][:, oc * 128:(oc + 1) * 128],
                                stream_bf[:, i * T + th * 512: i * T + (th + 1) * 512],
                                start=(i == 0), stop=False, skip_group_check=True)
                    return inst

                if i == 0:
                    mm_f = kb.op("tensor", mkf, deps=[wc, normB[i]],
                                 reads=(f"wbf{ws}", SB(i)),
                                 writes=tuple(PB(B0 + kk) for kk in range(4)))
                else:
                    mm_f = kb.op("tensor", mkf, deps=[wc, normB[i]] + kb.rdep(PB(B0)),
                                 reads=(f"wbf{ws}", SB(i)))
                    kb.did_write(mm_f, *[PB(B0 + kk) for kk in range(4)])

        if s_ == 0:
            emit_post_collective()

        def mkdelta(t, s_=s_, B0=B0):
            for oc in range(2):
                for th in range(2):
                    inst = t.matmul(
                        pb[B0 + oc * 2 + th][:, :],
                        Bm_bf[:, (s_ * 2 + oc) * 128:(s_ * 2 + oc + 1) * 128],
                        zp_bf[:, th * 512:(th + 1) * 512],
                        start=False, stop=True, skip_group_check=True)
            return inst

        mm_d = kb.op("tensor", mkdelta, deps=[mm_f, post_state["zp_done"], c_Bm], reads=("zp",))
        kb.did_write(mm_d, *[PB(B0 + kk) for kk in range(4)])

        for oc in range(2):
            o = s_ * 2 + oc
            ss = (s_ % 2) * 2 + oc
            v2 = kb.op("vector", (lambda v, o=o, oc=oc, slx=slx, B0=B0, ss=ss: [
                v.scalar_tensor_tensor(
                    fstage[ss][:, 0:512], pb[B0 + oc * 2][:, :], b_pm[:, o:o + 1],
                    xring[slx][:, oc * T:oc * T + 512], ALU.add, ALU.add),
                v.scalar_tensor_tensor(
                    fstage[ss][:, 512:1024], pb[B0 + oc * 2 + 1][:, :], b_pm[:, o:o + 1],
                    xring[slx][:, oc * T + 512:(oc + 1) * T], ALU.add, ALU.add)][-1]),
                deps=[mm_d, x2rd, dl["b"]],
                reads=(PB(B0 + oc * 2), PB(B0 + oc * 2 + 1), f"xr{slx}"),
                writes=(f"fst{ss}",))

            def make_ow(o=o, ss=ss, v2=v2):
                def emit():
                    return kb.dma("scalar", (lambda y, o=o, ss=ss: y.dma_start(
                        out=outT_d[o * 128:(o + 1) * 128, :], in_=fstage[ss][:, :])),
                        deps=[v2], reads=(f"fst{ss}",), sem=f"do{ss}")
                return emit
            pend_ow.append(make_ow())
    for owfn in pend_ow:
        out_steps.append(owfn())

    kb.op("sync", (lambda y: None), deps=out_steps)

    kb.emit(nc, sems)
    ctx.close()
    return nc


_SEL = np.concatenate([np.arange(g * 128, (g + 1) * 128) for g in range(0, 32, 4)])


def _stage_inputs(L0_attn_hidden_states, L0_attn_Wq, L0_attn_bq, L0_attn_Wo,
                  L0_attn_M, L0_attn_z, L0_ffn_W, L0_ffn_b, L0_ffn_A,
                  L0_ffn_Bmat, L0_ffn_LiMEs):
    x = np.asarray(L0_attn_hidden_states, dtype=np.float32).reshape(8192, D)
    xT = np.ascontiguousarray(x.T)
    Wq = np.asarray(L0_attn_Wq, np.float32)
    WqTsel = np.ascontiguousarray(Wq[_SEL, :].T)
    wq_st = np.ascontiguousarray(WqTsel.reshape(NI, 128, 4, 256).transpose(2, 0, 1, 3))
    bq = np.asarray(L0_attn_bq, np.float32)
    bq_pm = np.ascontiguousarray(bq[_SEL].reshape(8, 128).T)
    WoT = np.ascontiguousarray(np.asarray(L0_attn_Wo, np.float32).T)
    wo_st = np.ascontiguousarray(WoT.reshape(8, 4, 128, 16, 256).transpose(3, 0, 1, 2, 4)[:, :, [0, 2, 1, 3]])
    WT = np.ascontiguousarray(np.asarray(L0_ffn_W, np.float32).T)
    wt_st = np.ascontiguousarray(WT.reshape(NI, 128, 16, 256).transpose(2, 0, 1, 3))
    b_pm = np.ascontiguousarray(np.asarray(L0_ffn_b, np.float32).reshape(32, 128).T)
    wsl_pm = np.ascontiguousarray(
        WT[:, 0:8].reshape(NI, 128, 8).transpose(1, 0, 2).reshape(128, 256))
    zT = np.asarray(L0_attn_z, np.float32).T
    zrep = np.ascontiguousarray(np.broadcast_to(zT[:, :, None], (128, 8, 128)).reshape(128, 1024))
    M2 = np.ascontiguousarray(
        np.asarray(L0_attn_M, np.float32).transpose(1, 0, 2).reshape(128, 1024))
    A2 = np.ascontiguousarray(
        np.asarray(L0_ffn_A, np.float32).reshape(32, 128, 32).transpose(1, 0, 2).reshape(128, 1024))
    Bm = np.ascontiguousarray(np.asarray(L0_ffn_Bmat, np.float32))
    LiM = np.ascontiguousarray(np.asarray(L0_ffn_LiMEs, np.float32))
    LiMT = np.ascontiguousarray(LiM.T)
    shared = {"wq_st": wq_st, "wo_st": wo_st, "wt_st": wt_st, "bq_pm": bq_pm,
              "b_pm": b_pm, "zrep": zrep, "M2": M2, "A2": A2, "Bm": Bm,
              "LiM": LiM, "LiMT": LiMT, "Wsl_pm": wsl_pm,
              "ident_in": np.eye(128, dtype=np.float32)}
    in_maps = []
    for c in range(NCORES):
        m = dict(shared)
        m["xT"] = np.ascontiguousarray(xT[:, c * T:(c + 1) * T])
        in_maps.append(m)
    return in_maps


def kernel(L0_attn_hidden_states, L0_attn_Wq, L0_attn_bq, L0_attn_Wk,
           L0_attn_bk, L0_attn_Wv, L0_attn_bv, L0_attn_Wo, L0_attn_M,
           L0_attn_z, L0_ffn_W, L0_ffn_b, L0_ffn_A, L0_ffn_Bmat, L0_ffn_LiMEs):
    in_maps = _stage_inputs(L0_attn_hidden_states, L0_attn_Wq, L0_attn_bq,
                            L0_attn_Wo, L0_attn_M, L0_attn_z, L0_ffn_W,
                            L0_ffn_b, L0_ffn_A, L0_ffn_Bmat, L0_ffn_LiMEs)
    nc = build(debug=False)
    res = run_bass_kernel_spmd(nc, in_maps, core_ids=list(range(NCORES)))
    parts = [np.ascontiguousarray(res.results[c]["outT"].T) for c in range(NCORES)]
    out = np.concatenate(parts, axis=0).reshape(2, 4096, 4096)
    return out
